# version 47
# baseline (speedup 1.0000x reference)
import numpy as np
from contextlib import ExitStack

import concourse.bass as bass
import concourse.mybir as mybir
from concourse.bass_utils import run_bass_kernel_spmd
from concourse.bass_core_rust import IndirectOffsetOnAxis

F32 = mybir.dt.float32
BF16 = mybir.dt.bfloat16
I32 = mybir.dt.int32
AF = mybir.ActivationFunctionType
ALU = mybir.AluOpType
AX = mybir.AxisListType.X

NT = 4096
D = 1024
NH = 16
NC = 32
NG = 8
NE = 8
EPS_ATTN = 1e-6
EPS_LN = 1e-5
KCAP = 1536
NTB = 3
BIGOOB = 1.0e9

ENGINES = ("sync", "gpsimd", "scalar", "tensor", "vector")


_AT = ("t_ps", "s_tr0", "s_kvmm", "s_stA", "s_lgmm", "s_at", "s_nmm",
       "s_st2", "s_qmm", "s_tr2", "s_wout", "s_m5", "s_l1", "s_l2", "s_l3",
       "s_vmm", "s_b3m", "s_trc", "s_pfm", "s_rmm", "s_p3t", "s_p3t2")
_AV = ("v_ps", "vi", "s_zl", "s_sini", "s_stcp", "s_rZ", "s_kvev", "s_soadd",
       "s_mask", "s_stadd", "s_acc", "s_o5", "s_tev0", "s_tec", "s_ya",
       "s_b3c", "s_vev", "s_vsb", "s_rq", "s_zm", "s_tid", "s_ohb", "s_psb",
       "s_rsb", "s_p3a", "s_p3b", "s_p3w", "s_p3v", "s_p3d", "s_nmr",
       "s_lns", "s_lnm", "s_phf")
_AS = ("s_sq", "s_lnl", "s_rstd", "s_lnf", "s_phE", "s_lgE", "s_qphE",
       "s_e1", "s_e2", "s_e3", "s_div", "s_tev2")
ALIAS = {}
for n_ in _AT:
    ALIAS[n_] = "a_tensor"
for n_ in _AV:
    ALIAS[n_] = "a_vector"
for n_ in _AS:
    ALIAS[n_] = "a_scalar"


class Book:

    def __init__(self, nc, stack):
        self.nc = nc
        self.stack = stack
        self.h = {}
        self.n = {}
        self.lg = {}
        self.amap = {}

    def sem(self, name):
        name = ALIAS.get(name, name)
        if name not in self.h:
            self.h[name] = self.stack.enter_context(self.nc.semaphore(name))
            self.n[name] = 0
        return self.h[name]

    def v(self, name):
        return self.lg.get(name, 0)

    def abs_of(self, name, v):
        if name not in ALIAS:
            return name, v
        m = self.amap.get(name, {})
        if v not in m:
            raise KeyError(f"wait on {name}>={v} never recorded "
                           f"(logical count {self.lg.get(name, 0)})")
        return ALIAS[name], m[v]


class Prog:

    last_evt = {}

    def __init__(self, book):
        self.book = book
        self.ops = {e: [] for e in ENGINES}

    def op(self, eng, fn, waits=(), inc=None, inck=1, bare=False):
        wl = [("L", sn, v) for sn, v in waits]
        chain = (eng in ("vector", "scalar", "gpsimd")
                 and inck != 16 and not bare)
        if chain and eng in Prog.last_evt:
            wl.append(Prog.last_evt[eng])
        if inc is None and chain:
            inc = "a_" + eng
        incv = None
        if inc is not None:
            phys = ALIAS.get(inc, inc)
            self.book.sem(inc)
            self.book.n[phys] += inck
            self.book.lg[inc] = self.book.lg.get(inc, 0) + inck
            self.book.amap.setdefault(inc, {})[self.book.lg[inc]] =                 self.book.n[phys]
            incv = (phys, inck)
            if chain:
                Prog.last_evt[eng] = ("A", phys, self.book.n[phys])
        for _, sn, _v in wl:
            self.book.sem(sn if _ == "L" else sn)
        self.ops[eng].append((tuple(wl), fn, incv))
        return self.book.lg[inc] if inc else None

    def emit(self, nc):
        book = self.book

        def make(lst):
            resolved = []
            for waits, fn, incv in lst:
                rw = []
                for kind, sn, v in waits:
                    if v <= 0:
                        continue
                    if kind == "L":
                        rw.append(book.abs_of(sn, v))
                    else:
                        rw.append((sn, v))
                resolved.append((rw, fn, incv))

            def body(handle):
                last = {}
                for waits, fn, incv in resolved:
                    for s, v in waits:
                        if last.get(s, -1) >= v:
                            continue
                        handle.wait_ge(book.h[s], v)
                        last[s] = v
                    ins = fn(handle)
                    if incv is not None:
                        assert ins is not None
                        ins.then_inc(book.h[incv[0]], incv[1])
            return body

        with nc.Block() as blk:
            for eng in ENGINES:
                if self.ops[eng]:
                    getattr(blk, eng)(make(self.ops[eng]))


def fcl(f, **kw):
    return lambda eng, _f=f, _k=kw: _f(eng, **_k)


def build_nc(debug=False, stop_phase=5, a_chunks=NC, a_coll=True, a_parts='full', p3n=4):
    nc = bass.Bass("TRN2", target_bir_lowering=False)

    x_d = nc.dram_tensor("x", [NT, D], F32, kind="ExternalInput")
    wq_d = nc.dram_tensor("wq", [D, D], F32, kind="ExternalInput")
    wk_d = nc.dram_tensor("wk", [D, D], F32, kind="ExternalInput")
    wv_d = nc.dram_tensor("wv", [D, D], F32, kind="ExternalInput")
    wo_d = nc.dram_tensor("wo", [D, D], F32, kind="ExternalInput")
    gw_d = nc.dram_tensor("gw", [D, NE], F32, kind="ExternalInput")
    gb_d = nc.dram_tensor("gb", [NE, 1], F32, kind="ExternalInput")
    w1_d = nc.dram_tensor("w1", [NE, D, D], F32, kind="ExternalInput")
    w2_d = nc.dram_tensor("w2", [NE, D, D], F32, kind="ExternalInput")
    w3_d = nc.dram_tensor("w3", [NE, D, D], F32, kind="ExternalInput")
    b12_d = nc.dram_tensor("b12", [128, 128], F32, kind="ExternalInput")
    b3_d = nc.dram_tensor("b3", [1, NE * D], F32, kind="ExternalInput")
    psc_d = nc.dram_tensor("psc", [1, 1], F32, kind="ExternalInput")
    out_d = nc.dram_tensor("out", [NT, D], F32, kind="ExternalOutput")

    dbgk = "ExternalOutput" if debug else "Internal"
    kv_scr = nc.dram_tensor("kv_scr", [NC, 128, 2080], BF16, kind="Internal")
    acc_scr = nc.dram_tensor("acc_scr", [NC, 128, D], F32, kind=dbgk)
    exp_scr = nc.dram_tensor("exp_scr", [NE, NT], F32, kind=dbgk)
    ag_in = nc.dram_tensor("ag_in", [1, 66568], F32, kind="Internal")
    ag_out = nc.dram_tensor("ag_out", [2, 66568], F32, kind="Internal")
    v_in = nc.dram_tensor("v_in", [NE, D], F32, kind="Internal")
    v_out = nc.dram_tensor("v_out", [NE, D], F32, kind=dbgk)
    xn_tm = nc.dram_tensor("xn_tm", [NT, D], BF16, kind="Internal")
    w1b_d = nc.dram_tensor("w1b", [NE, D, D], BF16, kind="Internal")
    w2b_d = nc.dram_tensor("w2b", [NE, D, D], BF16, kind="Internal")
    w3b_d = nc.dram_tensor("w3b", [NE, D, D], BF16, kind="Internal")
    tok_scr = nc.dram_tensor("tok_scr", [NE, KCAP], I32, kind=dbgk)
    wtc_scr = nc.dram_tensor("wtc_scr", [NE, KCAP], BF16, kind=dbgk)
    xn_dbg = nc.dram_tensor("xn_dbg", [128, 8 * NT], BF16,
                            kind=dbgk) if debug else None

    stack = ExitStack()
    with stack:
        book = Book(nc, stack)
        V = book.v
        S = stack.enter_context

        ps = [S(nc.psum_tensor(f"ps{i}", [128, 512], F32)) for i in range(7)]
        psE = S(nc.psum_tensor("psE", [128, 1024], BF16))

        wpool = [S(nc.sbuf_tensor(f"wp{i}", [128, 8192], BF16)) for i in range(4)]
        gws = S(nc.sbuf_tensor("gws", [128, 64], BF16))
        gbs = S(nc.sbuf_tensor("gbs", [NE, 1], F32))
        identb = S(nc.sbuf_tensor("identb", [128, 128], BF16))
        identf = S(nc.sbuf_tensor("identf", [128, 128], F32))
        masktri = S(nc.sbuf_tensor("masktri", [128, 256], F32))
        onesrow = S(nc.sbuf_tensor("onesrow", [1, 128], BF16))
        psc128 = S(nc.sbuf_tensor("psc128", [128, 1], F32))
        pscb = S(nc.sbuf_tensor("pscb", [1, 1], BF16))
        triuSb = S(nc.sbuf_tensor("triuSb", [128, 128], BF16))
        onesM = S(nc.sbuf_tensor("onesM", [128, 128], BF16))
        iotaEr = S(nc.sbuf_tensor("iotaEr", [128, 8], F32))
        iotac = S(nc.sbuf_tensor("iotac", [128, 1], I32))
        p3c = S(nc.sbuf_tensor("p3c", [128, 48], F32))
        wt1b = S(nc.sbuf_tensor("wt1b", [128, 4], BF16))
        p3i = S(nc.sbuf_tensor("p3i", [128, 8], I32))
        posi = S(nc.sbuf_tensor("posi", [128, 4], I32))
        tid = S(nc.sbuf_tensor("tid", [128, 2], I32))
        indT = S(nc.sbuf_tensor("indT", [NE, NT], BF16))
        rp = [S(nc.sbuf_tensor(f"rp{i}", [128, 520], F32)) for i in range(2)]
        S_runb = S(nc.sbuf_tensor("S_runb", [128, 1040], BF16))
        zparts = S(nc.sbuf_tensor("zparts", [NE, 10], F32))
        rZ = S(nc.sbuf_tensor("rZ", [NE, 1], F32))
        zb2 = S(nc.sbuf_tensor("zb2", [NE, 2], F32))
        lnst = S(nc.sbuf_tensor("lnst", [128, 16], F32))
        dnm4 = S(nc.sbuf_tensor("dnm4", [128, 4], F32))
        rec4 = S(nc.sbuf_tensor("rec4", [128, 4], F32))
        p3b = S(nc.sbuf_tensor("p3b", [128, 224], F32))
        iota32 = S(nc.sbuf_tensor("iota32", [128, 32], F32))
        expst = S(nc.sbuf_tensor("expst", [NE, 512], F32))
        expld = S(nc.sbuf_tensor("expld", [NE, 512], F32))
        prT = S(nc.sbuf_tensor("prT", [NE, 512], F32))
        ohsb = S(nc.sbuf_tensor("ohsb", [128, 32], BF16))
        vsb = S(nc.sbuf_tensor("vsb", [NE, D], BF16))

        P = Prog(book)
        P.op("gpsimd", lambda e: e.iota(identb[:, :], [[1, 128]], channel_multiplier=-1,
                                        allow_small_or_imprecise_dtypes=True), inc="gi")
        P.op("gpsimd", lambda e: e.iota(identf[:, :], [[1, 128]],
                                        channel_multiplier=-1,
                                        allow_small_or_imprecise_dtypes=True), inc="gi")
        P.op("gpsimd", lambda e: e.iota(masktri[:, :].rearrange(
            "p (r f) -> p r f", f=128), [[0, 2], [1, 128]],
            channel_multiplier=-1,
            allow_small_or_imprecise_dtypes=True), inc="gi")
        P.op("gpsimd", lambda e: e.iota(iota32[:, :].rearrange(
            "p (k o) -> p k o", o=8), [[0, 4], [1, 8]], base=-8,
            channel_multiplier=0,
            allow_small_or_imprecise_dtypes=True), inc="gi")
        P.op("gpsimd", lambda e: e.memset(onesrow[:, :], 1.0), inc="gi")
        P.op("gpsimd", lambda e: e.iota(triuSb[:, :], [[1, 128]],
                                        channel_multiplier=-1,
                                        allow_small_or_imprecise_dtypes=True),
             inc="gi")
        P.op("gpsimd", lambda e: e.memset(onesM[:, :], 1.0), inc="gi")
        P.op("gpsimd", lambda e: e.iota(iotaEr[:, :], [[1, 8]],
                                        channel_multiplier=0,
                                        allow_small_or_imprecise_dtypes=True),
             inc="gi")
        P.op("gpsimd", lambda e: e.iota(iotac[:, :], [[0, 1]],
                                        channel_multiplier=1,
                                        allow_small_or_imprecise_dtypes=True),
             inc="gi")
        P.op("vector", lambda e: e.tensor_single_scalar(
            triuSb[:, :], triuSb[:, :], 0.0, ALU.is_gt),
            waits=[("gi", 6)], inc="vi")
        P.op("vector", lambda e: e.tensor_single_scalar(
            identb[:, :], identb[:, :], 0.0, ALU.is_equal),
            waits=[("gi", 1)], inc="vi")
        P.op("vector", lambda e: e.tensor_single_scalar(
            identf[:, :], identf[:, :], 0.0, ALU.is_equal),
            waits=[("gi", 2)], inc="vi")
        P.op("vector", lambda e: e.tensor_single_scalar(
            masktri[:, :], masktri[:, :], 0.0, ALU.is_ge),
            waits=[("gi", 3)], inc="vi")

        def wload8(sname, dst, src, ncols=1024, waits=()):
            for s in range(8):
                P.op("gpsimd", fcl(lambda e, s=0, dst=None, src=None, ncols=0:
                                   e.dma_start(
                                       out=dst[:, s * ncols:(s + 1) * ncols],
                                       in_=src[s * 128:(s + 1) * 128, :]),
                                   s=s, dst=dst, src=src, ncols=ncols),
                     waits=(waits if s == 0 else ()), inc=sname, inck=16)

        wload8("w_qk", wpool[2], wk_d)
        wload8("w_kv", wpool[3], wv_d)
        wload8("w_qk", wpool[1], wq_d)
        wload8("w_gw", gws, gw_d, ncols=8)
        P.op("gpsimd", lambda e: e.dma_start(out=gbs[:, :], in_=gb_d[:, :]),
             inc="w_gw", inck=16)
        P.op("gpsimd", lambda e: e.dma_start(out=pscb[:, :], in_=psc_d[:, :]),
             inc="w_ps", inck=16)
        P.op("tensor", lambda e: e.matmul(ps[0][:, 0:1], onesrow[:, :], pscb[:, :],
                                          start=True, stop=True),
             waits=[("w_ps", 16), ("gi", 5)], inc="t_ps")
        P.op("vector", lambda e: e.tensor_copy(psc128[:, :], ps[0][:, 0:1]),
             waits=[("t_ps", 1)], inc="v_ps")
        P.emit(nc)

        sAC = ExitStack()
        A = sAC.enter_context
        xnT = A(nc.sbuf_tensor("xnT", [128, 8 * NT], BF16))
        wos = A(nc.sbuf_tensor("wos", [128, 8192], BF16))
        pkv = [A(nc.sbuf_tensor(f"pkv{i}", [128, 2080], BF16)) for i in range(2)]
        xt = [A(nc.sbuf_tensor(f"xt{i}", [128, D], F32)) for i in range(2)]
        accst = A(nc.sbuf_tensor("accst", [128, D], F32))
        pre = A(nc.sbuf_tensor("pre", [128, D], BF16))
        preT = A(nc.sbuf_tensor("preT", [128, D], BF16))
        xnt = A(nc.sbuf_tensor("xnt", [128, 2 * D], BF16))
        atm = A(nc.sbuf_tensor("atm", [128, 512], BF16))
        S_own = [A(nc.sbuf_tensor(f"So{i}", [128, 520], F32)) for i in range(2)]
        S_par = A(nc.sbuf_tensor("S_par", [128, 520], F32))
        zi = A(nc.sbuf_tensor("zi", [8, 256], I32))

        qkT = wpool[0]
        wqs, wks, wvs = wpool[1], wpool[2], wpool[3]
        tmpe, tmpt = pre, preT

        P = Prog(book)
        P.op("gpsimd", lambda e: e.memset(
            pkv[0][:, 1024:2064].rearrange("p (h o) -> p h o", o=65)[:, :, 64:65],
            1.0), inc="gones")
        P.op("gpsimd", lambda e: e.memset(
            pkv[1][:, 1024:2064].rearrange("p (h o) -> p h o", o=65)[:, :, 64:65],
            1.0), inc="gones")
        P.op("vector", lambda e: e.memset(zi[:, :], 0), inc="s_zm")
        for kk in range(6):
            P.op("gpsimd", fcl(lambda e, kk=0: e.dma_start(
                out=tok_scr[:, kk * 256:(kk + 1) * 256], in_=zi[:, :]), kk=kk),
                waits=[("s_zm", 1)], inc="s_zf", inck=16)
            P.op("gpsimd", fcl(lambda e, kk=0: e.dma_start(
                out=wtc_scr[:, kk * 256:(kk + 1) * 256], in_=zi[:, :]), kk=kk),
                waits=[("s_zm", 1)], inc="s_zf", inck=16)

        KV_EVEN = [0, 1, 2, 3]
        KV_ODD = [4, 5, 6, 3]
        bank_free = {i: ([("v_ps", 1)] if i == 0 else []) for i in range(7)}
        ln_done = {}
        sq_v = {}
        lnf_v = {}

        for c in range(a_chunks):
            xb = c % 2
            B = xb * 8
            xw = []
            if c >= 2:
                lf, sq = ln_done[c - 2]
                xw = [("s_lnf", lf), ("s_sq", sq)]
            P.op("sync", fcl(lambda e, c=0, xb=0: e.dma_start(
                out=xt[xb][:, :], in_=x_d[c * 128:(c + 1) * 128, :]), c=c, xb=xb),
                waits=xw, inc=f"s_xt{xb}", inck=16)
            P.op("vector", fcl(lambda e, xb=0, B=0: e.reduce_sum(
                lnst[:, B:B + 1], xt[xb][:, :], axis=AX), xb=xb, B=B),
                waits=[(f"s_xt{xb}", V(f"s_xt{xb}"))])
            P.op("scalar", fcl(lambda e, xb=0, B=0: e.activation(
                xnt[:, xb * D:(xb + 1) * D], xt[xb][:, :], AF.Square,
                accum_out=lnst[:, B + 1:B + 2]), xb=xb, B=B),
                waits=[(f"s_xt{xb}", V(f"s_xt{xb}")), ("s_tr0", max(0, c - 1)),
                       (f"s_xns{xb}", 16 * (c // 2))],
                inc="s_sq")
            sq_v[c] = V("s_sq")
            P.op("vector", fcl(lambda e, B=0: e.tensor_single_scalar(
                lnst[:, B + 2:B + 3], lnst[:, B:B + 1], -1.0 / D, ALU.mult), B=B),
                waits=[("s_sq", sq_v[c])])
            P.op("vector", fcl(lambda e, B=0: e.tensor_tensor(
                lnst[:, B + 3:B + 4], lnst[:, B + 2:B + 3], lnst[:, B + 2:B + 3],
                ALU.mult), B=B))
            P.op("vector", fcl(lambda e, B=0: e.tensor_scalar(
                lnst[:, B + 4:B + 5], lnst[:, B + 1:B + 2], 1.0 / D, EPS_LN,
                ALU.mult, ALU.add), B=B))
            P.op("vector", fcl(lambda e, B=0: e.tensor_tensor(
                lnst[:, B + 5:B + 6], lnst[:, B + 4:B + 5], lnst[:, B + 3:B + 4],
                ALU.subtract), B=B), inc="s_lns")
            lnsv = V("s_lns")
            P.op("scalar", fcl(lambda e, B=0: e.activation(
                lnst[:, B + 6:B + 7], lnst[:, B + 5:B + 6], AF.Ln), B=B),
                waits=[("s_lns", lnsv)], inc="s_lnl")
            P.op("vector", fcl(lambda e, B=0: e.tensor_single_scalar(
                lnst[:, B + 3:B + 4], lnst[:, B + 6:B + 7], -0.5, ALU.mult), B=B),
                waits=[("s_lnl", V("s_lnl"))], inc="s_lnm")
            P.op("scalar", fcl(lambda e, B=0: e.activation(
                lnst[:, B + 7:B + 8], lnst[:, B + 3:B + 4], AF.Exp), B=B),
                waits=[("s_lnm", V("s_lnm"))], inc="s_rstd")
            P.op("vector", fcl(lambda e, B=0: e.tensor_tensor(
                lnst[:, B + 2:B + 3], lnst[:, B + 2:B + 3], lnst[:, B + 7:B + 8],
                ALU.mult), B=B),
                waits=[("s_rstd", V("s_rstd"))], inc="s_nmr")
            P.op("scalar", fcl(lambda e, xb=0, B=0: e.activation(
                xnt[:, xb * D:(xb + 1) * D], xt[xb][:, :], AF.Identity,
                bias=lnst[:, B + 2:B + 3], scale=lnst[:, B + 7:B + 8]),
                xb=xb, B=B),
                waits=[("s_nmr", V("s_nmr"))], inc="s_lnf")
            lnf_v[c] = V("s_lnf")
            ln_done[c] = (lnf_v[c], sq_v[c])
            P.op("gpsimd", fcl(lambda e, c=0, xb=0: e.dma_start(
                out=xn_tm[c * 128:(c + 1) * 128, :],
                in_=xnt[:, xb * D:(xb + 1) * D]), c=c, xb=xb),
                waits=[("s_lnf", lnf_v[c])], inc=f"s_xns{xb}", inck=16)
            if 16 <= c < 24:
                P.op("gpsimd", fcl(lambda e, s=0: e.dma_start(
                    out=wos[:, s * 1024:(s + 1) * 1024],
                    in_=wo_d[s * 128:(s + 1) * 128, :]), s=c - 16),
                    inc="w_wo", inck=16)
            for k in range(8):
                P.op("tensor", fcl(lambda e, k=0, xb=0: e.transpose(
                    psE[:, k * 128:(k + 1) * 128],
                    xnt[:, xb * D + k * 128: xb * D + (k + 1) * 128],
                    identb[:, :]), k=k, xb=xb),
                    waits=([("s_lnf", lnf_v[c]), ("s_tev0", c), ("vi", 1)]
                           if k == 0 else ()),
                    inc=("s_tr0" if k == 7 else None))
            if c >= 1:
                P.op("vector", fcl(lambda e, cc=0: e.tensor_copy(
                    xnT[:, :].rearrange("p (s t) -> p s t", s=8)
                    [:, :, cc * 128:(cc + 1) * 128],
                    psE[:, :].rearrange("p (s t) -> p s t", t=128)), cc=c - 1),
                    waits=[("s_tr0", c)], inc="s_tev0")
        P.op("vector", fcl(lambda e, cc=0: e.tensor_copy(
            xnT[:, :].rearrange("p (s t) -> p s t", s=8)
            [:, :, cc * 128:(cc + 1) * 128],
            psE[:, :].rearrange("p (s t) -> p s t", t=128)), cc=a_chunks - 1),
            waits=[("s_tr0", a_chunks)], inc="s_tev0")

        if a_parts != "p0":
            kvmm_v = {}
            phE_v = {}
            kvev_v = {}

            def p1_state(c):
                banks = KV_EVEN if c % 2 == 0 else KV_ODD
                pk = pkv[c % 2]
                for h in range(NH):
                    p = h // 2
                    bb = banks[0] if p < 4 else banks[1]
                    co = (p % 4) * 65
                    sw = []
                    if h == 0:
                        sw = [("s_kvev", kvev_v[c]), ("s_phE", phE_v[c]),
                              ("s_soadd", V("s_soadd"))]
                    P.op("tensor", fcl(lambda e, h=0, bb=0, co=0, pk=None:
                         e.matmul(ps[bb][(h % 2) * 64:(h % 2) * 64 + 64,
                                         co:co + 65],
                                  pk[:, h * 64:(h + 1) * 64],
                                  pk[:, 1024 + h * 65:1024 + h * 65 + 65],
                                  start=True, stop=True),
                         h=h, bb=bb, co=co, pk=pk),
                        waits=sw, inc=("s_stA" if h == 15 else None))
                stv = V("s_stA")
                cur, nxt = S_own[(c + 1) % 2], S_own[c % 2]
                if c == 0:
                    P.op("vector", fcl(lambda e, b0=0: e.tensor_copy(
                        S_own[0][:, 0:260], ps[b0][:, 0:260]), b0=banks[0]),
                        waits=[("s_stA", stv)])
                    P.op("vector", fcl(lambda e, b1=0: e.tensor_copy(
                        S_own[0][:, 260:520], ps[b1][:, 0:260]), b1=banks[1]),
                        inc="s_soadd")
                else:
                    P.op("vector", fcl(lambda e, b0=0, cur=None, nxt=None:
                         e.tensor_tensor(nxt[:, 0:260], cur[:, 0:260],
                                         ps[b0][:, 0:260], ALU.add),
                         b0=banks[0], cur=cur, nxt=nxt),
                        waits=[("s_stA", stv)])
                    P.op("vector", fcl(lambda e, b1=0, cur=None, nxt=None:
                         e.tensor_tensor(nxt[:, 260:520], cur[:, 260:520],
                                         ps[b1][:, 0:260], ALU.add),
                         b1=banks[1], cur=cur, nxt=nxt),
                        inc="s_soadd")
                sov = V("s_soadd")
                P.op("sync", fcl(lambda e, c=0, pk=None: e.dma_start(
                    out=kv_scr[c][:, 0:2064], in_=pk[:, 0:2064]), c=c, pk=pk),
                    waits=[("s_kvev", kvev_v[c])], inc=f"s_pkst{c % 2}",
                    inck=16)
                if c % 4 == 3:
                    g = c // 4
                    lb = (KV_EVEN if c % 2 == 0 else KV_ODD)[2]
                    lw = [("s_kvev", kvev_v[c]), ("w_gw", 144),
                          ("s_lgE", V("s_lgE"))]
                    for sidx in range(8):
                        P.op("tensor", fcl(lambda e, sidx=0, lb=0, g=0: e.matmul(
                            ps[lb][0:8, 0:512],
                            gws[:, sidx * 8:(sidx + 1) * 8],
                            xnT[:, sidx * NT + g * 512: sidx * NT + (g + 1) * 512],
                            start=(sidx == 0), stop=(sidx == 7)),
                            sidx=sidx, lb=lb, g=g),
                            waits=(lw if sidx == 0 else ()),
                            inc=("s_lgmm" if sidx == 7 else None))
                    P.op("scalar", fcl(lambda e, lb=0, g=0: e.activation(
                        expst[:, :], ps[lb][0:8, 0:512], AF.Exp, bias=gbs[:, :],
                        accum_out=zparts[:, g:g + 1]), lb=lb, g=g),
                        waits=[("s_lgmm", V("s_lgmm")), ("s_lgst", 16 * g)],
                        inc="s_lgE")
                    P.op("sync", fcl(lambda e, g=0: e.dma_start(
                        out=exp_scr[0:8, g * 512:(g + 1) * 512], in_=expst[:, :]),
                        g=g),
                        waits=[("s_lgE", V("s_lgE"))], inc="s_lgst", inck=16)
                    bank_free[lb] = bank_free[lb] + [("s_lgE", V("s_lgE"))]
                bank_free[banks[0]] = [("s_phE", phE_v[c] - 1),
                                       ("s_kvev", kvev_v[c]), ("s_soadd", sov)]
                bank_free[banks[1]] = [("s_phE", phE_v[c]),
                                       ("s_kvev", kvev_v[c]), ("s_soadd", sov)]
            for c in range(a_chunks):
                banks = KV_EVEN if c % 2 == 0 else KV_ODD
                pk = pkv[c % 2]
                wsrc = [(wks, 0, "w_qk", 256), (wks, 512, "w_qk", 256),
                        (wvs, 0, "w_kv", 128), (wvs, 512, "w_kv", 128)]
                kvmmv = []
                for ti in range(4):
                    wbuf, coff, wsem, wval = wsrc[ti]
                    bb = banks[ti]
                    fw = [("s_tev0", c + 1), (wsem, wval)] + bank_free[bb]
                    for sidx in range(8):
                        P.op("tensor", fcl(lambda e, sidx=0, bb=0, wbuf=None,
                                           coff=0, c=0: e.matmul(
                             ps[bb][:, :],
                             xnT[:, sidx * NT + c * 128: sidx * NT + (c + 1) * 128],
                             wbuf[:, sidx * 1024 + coff: sidx * 1024 + coff + 512],
                             start=(sidx == 0), stop=(sidx == 7)),
                             sidx=sidx, bb=bb, wbuf=wbuf, coff=coff, c=c),
                            waits=(fw if sidx == 0 else ()),
                            inc=("s_kvmm" if sidx == 7 else None))
                    kvmmv.append(V("s_kvmm"))
                if c >= 1:
                    p1_state(c - 1)
                phEl = []
                for ti in range(2):
                    bb = banks[ti]
                    half = ti * 512
                    ew = [("s_kvmm", kvmmv[ti]),
                          ("s_kvev", max(0, 4 * (c - 1) + ti + 1))]
                    P.op("scalar", fcl(lambda e, bb=0, half=0: e.activation(
                        tmpe[:, half:half + 512], ps[bb][:, :], AF.Exp),
                        bb=bb, half=half), waits=ew, inc="s_phE")
                    phEl.append(V("s_phE"))
                    P.op("vector", fcl(lambda e, bb=0, half=0: e.tensor_scalar(
                        tmpt[:, half:half + 512], ps[bb][:, :], 0.0, 1.0,
                        ALU.max, ALU.add), bb=bb, half=half),
                        waits=[("s_kvmm", kvmmv[ti]), ("s_phE", phEl[-1])])
                    pg = [("s_phE", phEl[-1])]
                    if ti == 0:
                        pg += [(f"s_pkst{c % 2}", 16 * (c // 2)),
                               ("s_stA", max(0, c - 1)), ("gones", 2)]
                    P.op("vector", fcl(lambda e, pk=None, half=0: e.tensor_tensor(
                        pk[:, half:half + 512], tmpe[:, half:half + 512],
                        tmpt[:, half:half + 512], ALU.min), pk=pk, half=half),
                        waits=pg, inc="s_kvev")
                for ti in range(2, 4):
                    bb = banks[ti]
                    h0 = (ti - 2) * 8
                    P.op("vector", fcl(lambda e, bb=0, h0=0, pk=None: e.tensor_copy(
                        pk[:, 1024:2064].rearrange("p (h o) -> p h o", o=65)
                        [:, h0:h0 + 8, 0:64],
                        ps[bb][:, :].rearrange("p (h o) -> p h o", o=64)),
                        bb=bb, h0=h0, pk=pk),
                        waits=[("s_kvmm", kvmmv[ti])], inc="s_kvev")
                kvmm_v[c] = kvmmv
                phE_v[c] = phEl[1]
                kvev_v[c] = V("s_kvev")
                bank_free[banks[2]] = [("s_kvev", kvev_v[c])]
                bank_free[banks[3]] = [("s_kvev", kvev_v[c])]
            p1_state(a_chunks - 1)

        if not a_coll:
            P.emit(nc)
            sAC.close()
            return nc
        P.op("vector", lambda e: e.reduce_sum(
            zparts[:, 8:9], zparts[:, 0:8], axis=AX),
            waits=[("s_lgE", V("s_lgE"))], inc="s_zl")
        fin = S_own[(NC - 1) % 2]
        P.op("sync", fcl(lambda e, fin=None: e.dma_start(
            out=ag_in[0:1, 0:66560].rearrange("a (p c) -> (a p) c", p=128),
            in_=fin[:, :]), fin=fin),
            waits=[("s_soadd", V("s_soadd"))], inc="s_agst", inck=16)
        P.op("sync", lambda e: e.dma_start(
            out=ag_in[0:1, 66560:66568].rearrange("a (p c) -> (a p) c", p=8),
            in_=zparts[:, 8:9]),
            waits=[("s_zl", 1)], inc="s_agst", inck=16)
        P.op("gpsimd", lambda e: e.collective_compute(
            "AllGather", ALU.bypass,
            replica_groups=[[0, 1], [2, 3], [4, 5], [6, 7]],
            ins=[ag_in.ap().opt()], outs=[ag_out.ap().opt()]),
            waits=[("s_agst", 32)], inc="s_cc1")
        P.op("gpsimd", lambda e: e.dma_start(
            out=S_par[:, :],
            in_=ag_out[0:1, 0:66560].rearrange("a (p c) -> (a p) c", p=128)),
            waits=[("s_cc1", 1)], inc="s_agld", inck=16)
        P.op("gpsimd", lambda e: e.dma_start(
            out=zb2[:, :],
            in_=ag_out[0:2, 66560:66568].rearrange("r p -> p r"),
            allow_slow_non_contiguous=True),
            inc="s_agld", inck=16)
        P.op("vector", lambda e: e.tensor_scalar(
            rp[0][:, :], S_par[:, :], psc128[:, :], None, ALU.mult),
            waits=[("s_agld", 32), ("v_ps", 1)], inc="s_sini")
        P.op("vector", lambda e: e.tensor_copy(S_runb[:, 0:520], rp[0][:, :]),
             inc="s_stcp")
        P.op("vector", lambda e: e.tensor_tensor(
            zb2[:, 0:1], zb2[:, 0:1], zb2[:, 1:2], ALU.add))
        P.op("vector", lambda e: e.reciprocal(rZ[:, :], zb2[:, 0:1]), inc="s_rZ")
        if debug:
            P.op("sync", lambda e: e.dma_start(out=xn_dbg.ap(), in_=xnT[:, :]),
                 waits=[("s_tev0", NC)], inc="s_dbg", inck=16)
        P.emit(nc)

        if stop_phase < 2:
            sAC.close()
            return nc
        P = Prog(book)
        wsrcs = {0: (w1_d, w1b_d), 1: (w2_d, w2b_d), 2: (w3_d, w3b_d)}
        cast_order = [(0, 0), (0, 1), (0, 2), (1, 0)]
        for ei in range(NE):
            if ei + 1 < NE:
                if (ei + 1, 1) not in cast_order:
                    cast_order.append((ei + 1, 1))
                cast_order.append((ei + 1, 2))
            if ei + 2 < NE:
                cast_order.append((ei + 2, 0))
        cast_dmas = []
        for ei, which in cast_order:
            src, dst = wsrcs[which]
            for c8 in range(8):
                cast_dmas.append(fcl(
                    lambda e, ei=0, c8=0, src=None, dst=None: e.dma_start(
                        out=dst[ei, c8 * 128:(c8 + 1) * 128, :],
                        in_=src[ei, c8 * 128:(c8 + 1) * 128, :]),
                    ei=ei, c8=c8, src=src, dst=dst))

        def emit_casts(n):
            while n > 0 and cast_dmas:
                P.op("gpsimd", cast_dmas.pop(0), inc="s_wc", inck=16)
                n -= 1

        wready = {}

        def wslot(ei, which):
            return (3 * ei + which + 2) % 4

        def emit_wload(ei, which, eng, waits):
            slot = wslot(ei, which)
            wsrc = (w1b_d, w2b_d, w3b_d)[which]
            for s in range(8):
                P.op(eng, fcl(lambda e, s=0, slot=0, ei=0, wsrc=None:
                     e.dma_start(out=wpool[slot][:, s * 1024:(s + 1) * 1024],
                                 in_=wsrc[ei, s * 128:(s + 1) * 128, :]),
                     s=s, slot=slot, ei=ei, wsrc=wsrc),
                     waits=(waits if s == 0 else ()),
                     inc=f"s_wl{slot}", inck=16)
            wready[(ei, which)] = (f"s_wl{slot}", V(f"s_wl{slot}"))
        p1f = [("s_phE", V("s_phE")), ("s_kvev", V("s_kvev")),
               ("s_soadd", V("s_soadd"))]
        free_at = {0: list(p1f), 1: list(p1f)}
        free_nm = {2: list(p1f), 3: list(p1f)}
        free_wo = {4: list(p1f), 5: list(p1f)}
        free_st6 = p1f + [("s_lgE", V("s_lgE"))]
        free_psE = [("s_tev0", NC)]
        pkv_free = {0: [("s_pkst0", 16 * (NC // 2)), ("s_stA", NC)],
                    1: [("s_pkst1", 16 * (NC // 2)), ("s_stA", NC)]}
        atm_free = {i: [] for i in range(4)}
        xt_free = {0: [("s_lnf", 31), ("s_sq", 31)],
                   1: [("s_lnf", 32), ("s_sq", 32)]}
        kvld_v = {}
        xld_v = {}

        def load_pkv(c):
            P.op("sync", fcl(lambda e, c=0: e.dma_start(
                out=pkv[c % 2][:, 0:2064], in_=kv_scr[c][:, 0:2064]), c=c),
                waits=[("s_pkst0", 16 * (NC // 2)),
                       ("s_pkst1", 16 * (NC // 2))] + pkv_free[c % 2],
                inc=f"s_kvld{c % 2}", inck=16)
            kvld_v[c] = (f"s_kvld{c % 2}", V(f"s_kvld{c % 2}"))

        def load_xt(c):
            P.op("sync", fcl(lambda e, c=0: e.dma_start(
                out=xt[c % 2][:, :], in_=x_d[c * 128:(c + 1) * 128, :]), c=c),
                waits=xt_free[c % 2], inc=f"s_xt{c % 2}", inck=16)
            xld_v[c] = (f"s_xt{c % 2}", V(f"s_xt{c % 2}"))

        load_pkv(0)
        load_pkv(1)
        load_xt(0)
        load_xt(1)

        prb = p3b[:, 0:32]
        eqb = p3b[:, 32:64]
        c1b = p3b[:, 64:96]
        oh1 = p3b[:, 96:128]
        oh2 = p3b[:, 128:160]
        p2t = p3b[:, 160:192]
        ohcat = p3b[:, 96:160].rearrange("p (b c) -> p b c", c=32)
        ohs = p3b[:, 192:224]
        mx4 = dnm4
        i14 = rec4

        def r3(ap):
            return ap.rearrange("p (k o) -> p k o", o=8)

        def col3(ap):
            return ap.rearrange("p (k o) -> p k o", o=1)

        def bc3(ap):
            return col3(ap).broadcast_to([128, 4, 8])

        P.op("vector", lambda e: e.memset(p3c[:, 0:8], 0.0), inc="s_rsb")
        zfv = V("s_zf")
        Rsb = p3c[:, 0:8]
        possb = p3c[:, 8:16]
        ohb_v = {}

        def emit_p3_a(g):
            P.op("sync", fcl(lambda e, g=0: e.dma_start(
                out=expld[:, :],
                in_=exp_scr[0:8, g * 512:(g + 1) * 512]), g=g),
                waits=[("s_p3a", g), ("s_lgst", 16 * NG)],
                inc="s_p3l", inck=16)
            P.op("vector", lambda e: e.tensor_scalar(
                prT[:, :], expld[:, :], rZ[:, :], None, ALU.mult),
                waits=[("s_p3l", V("s_p3l")), ("s_rZ", 1)], inc="s_p3a")

        def emit_p3_b(g):
            for t in range(4):
                P.op("tensor", fcl(lambda e, t=0: e.transpose(
                    ps[0][:, 128 + t * 8:128 + (t + 1) * 8],
                    prT[:, t * 128:(t + 1) * 128],
                    identf[0:8, 0:8]), t=t),
                    waits=([("s_p3a", V("s_p3a")), ("s_p3b", g), ("vi", 2)]
                           if t == 0 else ()),
                    inc=("s_p3t" if t == 3 else None))
            P.op("vector", lambda e: e.tensor_copy(prb[:, :], ps[0][:, 128:160]),
                 waits=[("s_p3t", V("s_p3t")), ("gi", 4)], inc="s_p3b")
            VV = lambda f, w=(), inc=None: P.op("vector", f, waits=w, inc=inc)
            VV(lambda e: e.tensor_reduce(col3(mx4), r3(prb), AX, ALU.max))
            VV(lambda e: e.tensor_tensor(r3(eqb), r3(prb), bc3(mx4), ALU.is_equal))
            VV(lambda e: e.tensor_tensor(c1b, eqb, iota32[:, :], ALU.mult))
            VV(lambda e: e.tensor_reduce(col3(i14), r3(c1b), AX, ALU.min))
            VV(lambda e: e.tensor_tensor(r3(oh1), r3(iota32[:, :]), bc3(i14),
                                         ALU.is_equal))
            VV(lambda e: e.tensor_tensor(p2t, prb, oh1, ALU.subtract))
            VV(lambda e: e.tensor_reduce(col3(mx4), r3(p2t), AX, ALU.max))
            VV(lambda e: e.tensor_tensor(r3(eqb), r3(p2t), bc3(mx4), ALU.is_equal))
            VV(lambda e: e.tensor_tensor(c1b, eqb, iota32[:, :], ALU.mult))
            VV(lambda e: e.tensor_reduce(col3(i14), r3(c1b), AX, ALU.min))
            VV(lambda e: e.tensor_tensor(r3(oh2), r3(iota32[:, :]), bc3(i14),
                                         ALU.is_equal))
            VV(lambda e: e.tensor_tensor(ohs, oh1, oh2, ALU.add), inc="s_p3w")
            P.op("vector", fcl(lambda e, g=0: e.tensor_copy(
                ohsb[:, :], ohs[:, :]), g=g),
                waits=([("s_pfm", 4 * g), ("s_rmm", 4 * g)]
                       if p3n >= 4 else []), inc="s_ohb")
            ohb_v[g] = V("s_ohb")

        def emit_p3_c(g):
            for t in range(4):
                P.op("tensor", fcl(lambda e, t=0: e.transpose(
                    ps[3][0:8, t * 128:(t + 1) * 128],
                    ohs[:, t * 8:(t + 1) * 8], identf[:, :]), t=t),
                    waits=([("s_p3w", V("s_p3w")), ("s_p3d", 4 * g),
                            ("s_div", V("s_div")), ("s_rq", V("s_rq"))]
                           if t == 0 else ()),
                    inc="s_p3t2")
                P.op("vector", fcl(lambda e, g=0, t=0: e.tensor_copy(
                    indT[:, g * 512 + t * 128:g * 512 + (t + 1) * 128],
                    ps[3][0:8, t * 128:(t + 1) * 128]),
                    g=g, t=t),
                    waits=[("s_p3t2", V("s_p3t2"))], inc="s_p3d")
            free_nm[3] = free_nm[3] + [("s_p3d", V("s_p3d"))]
            for t in range(4):
                P.op("tensor", fcl(lambda e, t=0: e.matmul(
                    ps[0][:, 160 + t * 8:160 + t * 8 + 8],
                    triuSb[:, :], ohsb[:, t * 8:(t + 1) * 8],
                    start=True, stop=True), t=t),
                    waits=[("s_ohb", ohb_v[g]), ("vi", 4),
                           ("s_psb", max(0, 4 * (g - 1) + t + 1))],
                    inc="s_pfm")
                P.op("tensor", fcl(lambda e, t=0: e.matmul(
                    ps[0][:, 192 + t * 8:192 + t * 8 + 8],
                    onesM[:, :], ohsb[:, t * 8:(t + 1) * 8],
                    start=True, stop=True), t=t),
                    waits=[("gi", 7),
                           ("s_rsb", max(0, 1 + 4 * (g - 1) + t + 1)
                            if g > 0 else 0)],
                    inc="s_rmm")

        def emit_p3_tile(g, t):
            T = g * 4 + t
            P.op("vector", fcl(lambda e, t=0: e.tensor_tensor(
                possb, ps[0][:, 160 + t * 8:160 + t * 8 + 8], Rsb, ALU.add),
                t=t),
                waits=[("s_pfm", 4 * g + t + 1), ("s_rsb", V("s_rsb"))],
                inc="s_psb")
            P.op("vector", fcl(lambda e, t=0: e.tensor_tensor(
                Rsb, Rsb, ps[0][:, 192 + t * 8:192 + t * 8 + 8], ALU.add),
                t=t),
                 waits=[("s_rmm", 4 * g + t + 1)], inc="s_rsb")
            tp = T % 2
            P.op("vector", fcl(lambda e, T=0, tp=0: e.tensor_single_scalar(
                tid[:, tp:tp + 1], iotac[:, :], T * 128, ALU.add),
                T=T, tp=tp),
                waits=[("gi", 9),
                       (f"s_sct{tp}", V(f"s_sct{tp}"))],
                inc="s_tid")
            sl = slice(t * 8, (t + 1) * 8)
            tm2 = p3c[:, 16:32].rearrange("p (k o) -> p k o", o=8)
            ohc = ohcat[:, :, sl]

            def bc2(ap):
                return ap.rearrange("p (o c) -> p o c", o=1).broadcast_to(
                    [128, 2, 8])

            def c2(lo):
                return p3c[:, lo:lo + 2].rearrange("p (k o) -> p k o", o=1)

            P.op("vector", fcl(lambda e, ohc=None: e.tensor_tensor(
                tm2, ohc, bc2(possb), ALU.mult), ohc=ohc))
            P.op("vector", lambda e: e.tensor_reduce(c2(32), tm2, AX, ALU.add))
            P.op("vector", fcl(lambda e, ohc=None: e.tensor_tensor(
                tm2, ohc, bc2(iotaEr[:, :]), ALU.mult), ohc=ohc))
            P.op("vector", lambda e: e.tensor_reduce(c2(34), tm2, AX, ALU.add))
            P.op("vector", fcl(lambda e, ohc=None, sl=None: e.tensor_tensor(
                tm2, ohc, bc2(prb[:, sl]), ALU.mult), ohc=ohc, sl=sl))
            P.op("vector", lambda e: e.tensor_reduce(c2(36), tm2, AX, ALU.add))
            P.op("vector", fcl(lambda e, tp=0: e.tensor_copy(
                wt1b[:, tp * 2:tp * 2 + 2], p3c[:, 36:38]), tp=tp))
            P.op("vector", lambda e: e.tensor_single_scalar(
                p3c[:, 38:40], p3c[:, 32:34], float(KCAP), ALU.is_ge))
            P.op("vector", lambda e: e.tensor_copy(
                p3i[:, 0:2], p3c[:, 32:34]))
            P.op("vector", lambda e: e.tensor_single_scalar(
                p3i[:, 2:4], p3i[:, 0:2], 127, ALU.bitwise_and))
            P.op("vector", lambda e: e.tensor_single_scalar(
                p3i[:, 4:6], p3i[:, 0:2], 7, ALU.arith_shift_right))
            P.op("vector", lambda e: e.tensor_copy(
                p3c[:, 40:42], p3i[:, 2:4]))
            P.op("vector", lambda e: e.tensor_copy(
                p3c[:, 42:44], p3i[:, 4:6]))
            P.op("vector", lambda e: e.scalar_tensor_tensor(
                p3c[:, 44:46], p3c[:, 40:42], 12.0, p3c[:, 42:44],
                ALU.mult, ALU.add))
            P.op("vector", lambda e: e.tensor_scalar(
                p3c[:, 46:48], p3c[:, 34:36], float(KCAP), None,
                ALU.mult))
            P.op("vector", lambda e: e.tensor_tensor(
                p3c[:, 46:48], p3c[:, 46:48], p3c[:, 44:46], ALU.add))
            P.op("vector", lambda e: e.scalar_tensor_tensor(
                p3c[:, 46:48], p3c[:, 38:40], BIGOOB, p3c[:, 46:48],
                ALU.mult, ALU.add))
            P.op("vector", fcl(lambda e, tp=0: e.tensor_copy(
                posi[:, tp * 2:tp * 2 + 2], p3c[:, 46:48]),
                tp=tp), inc="s_p3v")
            p3vv = V("s_p3v")
            for k in range(2):
                P.op("gpsimd", fcl(lambda e, k=0, tp=0: e.indirect_dma_start(
                    out=tok_scr.ap().rearrange("e k -> (e k)").unsqueeze(1),
                    out_offset=IndirectOffsetOnAxis(
                        posi[:, tp * 2 + k:tp * 2 + k + 1], 0),
                    in_=tid[:, tp:tp + 1], in_offset=None, oob_is_err=False),
                    k=k, tp=tp),
                    waits=[("s_zf", zfv), ("s_p3v", p3vv),
                           ("s_tid", V("s_tid"))],
                    inc=f"s_sct{tp}", inck=16)
                P.op("gpsimd", fcl(lambda e, k=0, tp=0: e.indirect_dma_start(
                    out=wtc_scr.ap().rearrange("e k -> (e k)").unsqueeze(1),
                    out_offset=IndirectOffsetOnAxis(
                        posi[:, tp * 2 + k:tp * 2 + k + 1], 0),
                    in_=wt1b[:, tp * 2 + k:tp * 2 + k + 1],
                    in_offset=None, oob_is_err=False),
                    k=k, tp=tp),
                    waits=[("s_zf", zfv), ("s_p3v", p3vv)],
                    inc=f"s_sct{tp}", inck=16)
            gd = [("s_psb", V("s_psb")), ("s_rsb", V("s_rsb"))]
            free_wo[4] = free_wo[4] + gd
            free_wo[5] = free_wo[5] + gd

        for g in range(NG):
            qb = [2, 3, 4, 5]
            qev0 = V("s_qev")
            for i in range(16):
                b = qb[i % 4]
                wbuf = wqs if i < 8 else wks
                ft = i % 8
                fw = [("s_qev", qev0 + i - 3)]
                if i < 4:
                    fw += (free_nm[b] if b in (2, 3) else free_wo[b])
                for s in range(8):
                    P.op("tensor", fcl(lambda e, s=0, b=0, wbuf=None, ft=0, g=0:
                         e.matmul(ps[b][:, :],
                                  wbuf[:, s * 1024 + ft * 128: s * 1024 + ft * 128 + 128],
                                  xnT[:, s * NT + g * 512: s * NT + (g + 1) * 512],
                                  start=(s == 0), stop=(s == 7)),
                         s=s, b=b, wbuf=wbuf, ft=ft, g=g),
                        waits=(fw if s == 0 else ()),
                        inc=("s_qmm" if s == 7 else None))
                qmmv = V("s_qmm")
                half = (i % 2) * 512
                dst = (i % 8) * 512 + (0 if i < 8 else 4096)
                ew = [("s_qmm", qmmv), ("s_qev", qev0 + i - 1)]
                if i == 0:
                    ew.append(("s_phf", g) if g > 0 else ("s_kvev", 4 * NC))
                P.op("scalar", fcl(lambda e, b=0, half=0: e.activation(
                    tmpe[:, half:half + 512], ps[b][:, :], AF.Exp),
                    b=b, half=half), waits=ew, inc="s_qphE")
                P.op("vector", fcl(lambda e, b=0, half=0: e.tensor_scalar(
                    tmpt[:, half:half + 512], ps[b][:, :], 0.0, 1.0,
                    ALU.max, ALU.add), b=b, half=half),
                    waits=[("s_qmm", qmmv), ("s_qphE", V("s_qphE"))] +
                          ([("s_phf", g)] if (g > 0 and i == 0) else []))
                P.op("vector", fcl(lambda e, half=0, dst=0: e.tensor_tensor(
                    qkT[:, dst:dst + 512], tmpe[:, half:half + 512],
                    tmpt[:, half:half + 512], ALU.min), half=half, dst=dst),
                    waits=[("s_qphE", V("s_qphE"))], inc="s_qev")
            qphi = V("s_qev")
            free_nm = {2: [("s_qev", qev0 + 13)], 3: [("s_qev", qev0 + 14)]}
            free_wo = {4: [("s_qev", qev0 + 15)], 5: [("s_qev", qev0 + 16)]}

            for cc in range(4):
                c = g * 4 + cc
                tl = cc * 128
                pkvb = pkv[c % 2]
                kvv = kvld_v[c]
                xvv = xld_v[c]
                nmm_prev = V("s_nmm")

                def qo(h, base=0, tl=tl):
                    return base + (h // 2) * 512 + tl

                at_v = {}
                nmm_at_inter = {}

                def emit_AT(p):
                    for h in (2 * p, 2 * p + 1):
                        b = h % 2
                        P.op("tensor", fcl(lambda e, h=0, b=0, q=0, k=0: e.matmul(
                            ps[b][:, 0:128],
                            qkT[(h % 2) * 64:(h % 2) * 64 + 64, k:k + 128],
                            qkT[(h % 2) * 64:(h % 2) * 64 + 64, q:q + 128],
                            start=True, stop=True),
                            h=h, b=b, q=qo(h), k=qo(h, 4096)),
                            waits=[("s_qev", qphi)] + free_at[b],
                            inc="s_at")
                        at_v[h] = V("s_at")

                def emit_mask(p):
                    for h in (2 * p, 2 * p + 1):
                        mw = [("s_at", at_v[h]), ("vi", 3)]
                        if h >= 4:
                            mw += atm_free[h % 4]
                        P.op("vector", fcl(lambda e, h=0: e.tensor_tensor(
                            atm[:, (h % 4) * 128:(h % 4) * 128 + 128],
                            ps[h % 2][:, 0:128], masktri[:, 0:128], ALU.mult), h=h),
                            waits=mw, inc="s_mask")
                        free_at[h % 2] = [("s_mask", V("s_mask"))]

                def emit_numer(p):
                    for h in (2 * p, 2 * p + 1):
                        nb = 2 + ((h // 4) % 2)
                        co = (h % 4) * 65
                        nw = [("s_mask", V("s_mask")), kvv,
                              ("s_stcp", 1 + c)]
                        if h % 4 == 0:
                            nw += free_nm[nb]
                        P.op("tensor", fcl(lambda e, h=0, nb=0, co=0, pkvb=None:
                             e.matmul(ps[nb][:, co:co + 65],
                                      atm[:, (h % 4) * 128:(h % 4) * 128 + 128],
                                      pkvb[:, 1024 + h * 65:1024 + h * 65 + 65],
                                      start=True, stop=False),
                             h=h, nb=nb, co=co, pkvb=pkvb),
                            waits=nw)
                        P.op("tensor", fcl(lambda e, h=0, nb=0, co=0, q=0,
                                           so=0: e.matmul(
                            ps[nb][:, co:co + 65],
                            qkT[(h % 2) * 64:(h % 2) * 64 + 64, q:q + 128],
                            S_runb[(h % 2) * 64:(h % 2) * 64 + 64,
                                   so + (h // 2) * 65:so + (h // 2) * 65 + 65],
                            start=False, stop=True),
                            h=h, nb=nb, co=co, q=qo(h), so=(c % 2) * 520),
                            inc="s_nmm")
                        nmm_at_inter[h] = V("s_nmm")
                        atm_free[h % 4] = [("s_nmm", V("s_nmm"))]

                def emit_state(p):
                    for h in (2 * p, 2 * p + 1):
                        sb = 6 if p < 4 else 5
                        co = (p % 4) * 65
                        sw = [kvv]
                        if h == 0:
                            sw += free_st6
                        if h == 8:
                            sw += free_wo[5]
                        P.op("tensor", fcl(lambda e, h=0, sb=0, co=0, pkvb=None:
                             e.matmul(ps[sb][(h % 2) * 64:(h % 2) * 64 + 64,
                                             co:co + 65],
                                      pkvb[:, h * 64:(h + 1) * 64],
                                      pkvb[:, 1024 + h * 65:1024 + h * 65 + 65],
                                      start=True, stop=True),
                             h=h, sb=sb, co=co, pkvb=pkvb),
                            waits=sw, inc="s_st2")

                def emit_div(grp):
                    nb = 2 + (grp % 2)
                    nv = nmm_at_inter[grp * 4 + 3]
                    P.op("vector", fcl(lambda e, nb=0: e.tensor_single_scalar(
                        dnm4[:, :].rearrange("p (h o) -> p h o", o=1),
                        ps[nb][:, 0:260].rearrange("p (h o) -> p h o", o=65)
                        [:, :, 64:65],
                        EPS_ATTN, ALU.add), nb=nb),
                        waits=[("s_nmm", nv)])
                    P.op("vector", lambda e: e.reciprocal(rec4[:, :], dnm4[:, :]),
                         waits=[("s_div", V("s_div"))], inc="s_rq")
                    for hh in range(4):
                        h = grp * 4 + hh
                        P.op("scalar", fcl(lambda e, h=0, hh=0, nb=0:
                             e.activation(
                                 pre[:, h * 64:(h + 1) * 64],
                                 ps[nb][:, hh * 65:hh * 65 + 64],
                                 AF.Identity, scale=rec4[:, hh:hh + 1]),
                             h=h, hh=hh, nb=nb),
                            waits=[("s_rq", V("s_rq"))], inc="s_div")
                    free_nm[nb] = [("s_div", V("s_div"))]

                emit_AT(0)
                for p in range(8):
                    emit_mask(p)
                    if p + 1 < 8:
                        emit_AT(p + 1)
                    emit_numer(p)
                    emit_state(p)
                    if p % 2 == 1:
                        emit_div(p // 2)
                st2v = V("s_st2")
                cur, nxt = rp[c % 2], rp[(c + 1) % 2]
                P.op("vector", fcl(lambda e, cur=None, nxt=None: e.tensor_tensor(
                    nxt[:, 0:260], cur[:, 0:260], ps[6][:, 0:260], ALU.add),
                    cur=cur, nxt=nxt),
                    waits=[("s_st2", st2v), ("s_sini", 1)])
                P.op("vector", fcl(lambda e, cur=None, nxt=None: e.tensor_tensor(
                    nxt[:, 260:520], cur[:, 260:520], ps[5][:, 0:260], ALU.add),
                    cur=cur, nxt=nxt),
                    inc="s_stadd")
                free_st6 = [("s_stadd", V("s_stadd"))]
                P.op("vector", fcl(lambda e, nxt=None, so=0: e.tensor_copy(
                    S_runb[:, so:so + 520], nxt[:, :]),
                    nxt=nxt, so=((c + 1) % 2) * 520),
                    waits=[("s_nmm", nmm_prev)], inc="s_stcp")
                divv = V("s_div")
                for k in range(8):
                    P.op("tensor", fcl(lambda e, k=0: e.transpose(
                        psE[:, k * 128:(k + 1) * 128], pre[:, k * 128:(k + 1) * 128],
                        identb[:, :]), k=k),
                        waits=(([("s_div", divv)] + free_psE) if k == 0 else ()),
                        inc=("s_tr2" if k == 7 else None))
                P.op("scalar", lambda e: e.activation(
                    preT[:, :], psE[:, :], AF.Identity),
                     waits=[("s_tr2", V("s_tr2"))], inc="s_tev2")
                free_psE = [("s_tev2", V("s_tev2"))]
                tev2 = V("s_tev2")
                for half in range(2):
                    wb = 4 + half
                    ww = [("s_tev2", tev2), ("w_wo", 128)] + free_wo[wb]
                    if half == 1:
                        ww.append(("s_stadd", V("s_stadd")))
                    for s in range(8):
                        P.op("tensor", fcl(lambda e, s=0, wb=0, half=0: e.matmul(
                            ps[wb][:, :],
                            preT[:, s * 128:(s + 1) * 128],
                            wos[:, s * 1024 + half * 512: s * 1024 + half * 512 + 512],
                            start=(s == 0), stop=(s == 7)),
                            s=s, wb=wb, half=half),
                            waits=(ww if s == 0 else ()),
                            inc=("s_wout" if s == 7 else None))
                woutv = V("s_wout")
                P.op("vector", fcl(lambda e, cb=0: e.tensor_tensor(
                    accst[:, 0:512], ps[4][:, :], xt[cb][:, 0:512], ALU.add),
                    cb=c % 2),
                    waits=[("s_wout", woutv - 1), xvv,
                           ("s_acst", 16 * c)])
                P.op("vector", fcl(lambda e, cb=0: e.tensor_tensor(
                    accst[:, 512:1024], ps[5][:, :], xt[cb][:, 512:1024], ALU.add),
                    cb=c % 2),
                    waits=[("s_wout", woutv)], inc="s_acc")
                accv = V("s_acc")
                free_wo = {4: [("s_acc", accv)], 5: [("s_acc", accv)]}
                P.op("gpsimd", fcl(lambda e, c=0: e.dma_start(
                    out=acc_scr[c], in_=accst[:, :]), c=c),
                    waits=[("s_acc", accv)], inc="s_acst", inck=16)
                pkv_free[c % 2] = [("s_nmm", V("s_nmm")), ("s_st2", V("s_st2"))]
                xt_free[c % 2] = [("s_acc", accv)]
                if c + 2 < NC:
                    load_pkv(c + 2)
                    load_xt(c + 2)
                if stop_phase >= 3:
                    if g < 7:
                        if cc == 0 and p3n >= 1:
                            emit_p3_a(g)
                        elif cc == 1 and p3n >= 2:
                            emit_p3_b(g)
                        elif cc == 2 and p3n >= 3:
                            emit_p3_c(g)
                        elif cc == 3 and p3n >= 4:
                            emit_p3_tile(g, 0)
                            emit_p3_tile(g, 1)
                    else:
                        if cc == 0 and p3n >= 2:
                            emit_p3_b(g)
                        elif cc == 1 and p3n >= 3:
                            emit_p3_c(g)
                        elif cc == 2 and p3n >= 4:
                            emit_p3_tile(g, 0)
                            emit_p3_tile(g, 1)
                        elif cc == 3 and p3n >= 4:
                            emit_p3_tile(g, 2)
                            emit_p3_tile(g, 3)
                emit_casts(7)
                if g == 7 and cc == 1:
                    emit_wload(0, 0, "gpsimd",
                               [("s_qmm", V("s_qmm")), ("s_wc", 16 * 8)])
                if g == 7 and cc == 2:
                    emit_wload(1, 0, "gpsimd",
                               [("s_qmm", V("s_qmm")), ("s_wc", 16 * 32)])
            if stop_phase >= 3 and p3n >= 4 and g < 7:
                emit_p3_tile(g, 2)
                emit_p3_tile(g, 3)
                if g == 6 and p3n >= 1:
                    emit_p3_a(7)
            if g == 6:
                emit_wload(0, 1, "gpsimd",
                           [("s_kvmm", V("s_kvmm")), ("s_wc", 16 * 16)])
            P.op("vector", lambda e: e.tensor_copy(lnst[:, 3:4], lnst[:, 2:3]),
                 waits=[("s_tr2", V("s_tr2")), ("s_wout", V("s_wout"))],
                 inc="s_phf")
        P.emit(nc)
        sAC.close()


        if stop_phase < 4:
            return nc
        sDE = ExitStack()
        DD = sDE.enter_context
        xnTc = DD(nc.sbuf_tensor("xnTc", [128, 8192], BF16))
        xng = DD(nc.sbuf_tensor("xng", [128, 4096], BF16))
        idxs = DD(nc.sbuf_tensor("idxs", [128, 24], I32))
        wtcs = DD(nc.sbuf_tensor("wtcs", [128, 24], BF16))
        h1T = DD(nc.sbuf_tensor("h1T", [128, 4096], BF16))
        h2T = DD(nc.sbuf_tensor("h2T", [128, 4096], BF16))
        ybuf = DD(nc.sbuf_tensor("ybuf", [128, 1024], BF16))
        b12s = DD(nc.sbuf_tensor("b12s", [128, 128], F32))
        b3s = DD(nc.sbuf_tensor("b3s", [1, 1024], BF16))
        v32 = DD(nc.sbuf_tensor("v32", [NE, D], F32))
        vstg = DD(nc.sbuf_tensor("vstg", [1, 1024], F32))
        b3bc = DD(nc.sbuf_tensor("b3bc", [128, 1024], F32))
        accld = DD(nc.sbuf_tensor("accld", [128, 4096], F32))
        outst = DD(nc.sbuf_tensor("outst", [128, 4096], F32))
        slots = wpool

        P = Prog(book)
        pcd = [("s_acst", 16 * NC), ("s_wout", V("s_wout")),
               ("s_acc", V("s_acc")), ("s_nmm", V("s_nmm")),
               ("s_qmm", V("s_qmm")), ("s_at", V("s_at")),
               ("s_kvmm", V("s_kvmm")), ("s_lgmm", V("s_lgmm")),
               ("s_st2", V("s_st2")), ("s_tr2", V("s_tr2")),
               ("s_tr0", V("s_tr0")), ("s_sq", V("s_sq")),
               ("s_p3d", V("s_p3d")), ("s_p3t", V("s_p3t")), ("s_p3t2", V("s_p3t2")),
               ("s_p3a", V("s_p3a")), ("s_agst", 32), ("s_sini", 1),
               ("s_lgst", 16 * NG), ("s_p3l", V("s_p3l")),
               ("s_xns0", 16 * (NC // 2)), ("s_xns1", 16 * (NC // 2)),
               ("s_kvld0", V("s_kvld0")),
               ("s_kvld1", V("s_kvld1")), ("s_xt0", V("s_xt0")),
               ("s_xt1", V("s_xt1")), ("s_pkst0", 16 * (NC // 2)),
               ("s_pkst1", 16 * (NC // 2))]
        P.op("sync", lambda e: e.dma_start(out=b12s[:, :], in_=b12_d[:, :]),
             waits=pcd, inc="s_bld", inck=16)
        def load_b3(ei, waits=()):
            P.op("gpsimd", fcl(lambda e, ei=0: e.dma_start(
                out=b3s[:, :], in_=b3_d[0:1, ei * 1024:(ei + 1) * 1024]), ei=ei),
                waits=waits, inc="s_bl3", inck=16)
        load_b3(0, waits=pcd)

        slot_free = {
            0: [("s_nmm", V("s_nmm")), ("s_at", V("s_at"))],
            1: [("s_qmm", V("s_qmm"))],
            2: [("s_qmm", V("s_qmm"))],
            3: [("s_kvmm", V("s_kvmm"))],
        }

        def load_w(ei, which, wsrc):
            slot = wslot(ei, which)
            for s in range(8):
                P.op("sync", fcl(lambda e, s=0, slot=0, ei=0, wsrc=None:
                     e.dma_start(out=slots[slot][:, s * 1024:(s + 1) * 1024],
                                 in_=wsrc[ei, s * 128:(s + 1) * 128, :]),
                     s=s, slot=slot, ei=ei, wsrc=wsrc),
                     waits=((slot_free[slot] +
                             [("s_wc", 16 * 8 *
                               (cast_order.index((ei, which)) + 1))])
                            if s == 0 else ()),
                     inc=f"s_wl{slot}", inck=16)
            wready[(ei, which)] = (f"s_wl{slot}", V(f"s_wl{slot}"))

        load_w(0, 2, w3b_d)

        idx_v = {}
        wtc_v = {}

        def load_idx(e_):
            eb = e_ % 2
            gw_ = [("s_sct0", 16 * 64), ("s_sct1", 16 * 64)] + \
                (pcd if e_ < 2 else []) + \
                (gth_hist[e_ - 2] if e_ >= 2 else [])
            P.op("sync", fcl(lambda e, e_=0, eb=0: e.dma_start(
                out=idxs[:, eb * 12:(eb + 1) * 12],
                in_=tok_scr[e_:e_ + 1, :].rearrange(
                    "a (t k) -> (a t) k", k=12)), e_=e_, eb=eb),
                waits=gw_, inc=f"s_idx{eb}", inck=16)
            idx_v[e_] = (f"s_idx{eb}", V(f"s_idx{eb}"))
            vw_ = [("s_sct0", 16 * 64), ("s_sct1", 16 * 64)] + \
                (pcd if e_ < 2 else []) + \
                ([("s_vmm", vmm_hist[e_ - 2])] if e_ >= 2 else [])
            P.op("sync", fcl(lambda e, e_=0, eb=0: e.dma_start(
                out=wtcs[:, eb * 12:(eb + 1) * 12],
                in_=wtc_scr[e_:e_ + 1, :].rearrange(
                    "a (t k) -> (a t) k", k=12)), e_=e_, eb=eb),
                waits=vw_, inc=f"s_wtc{eb}", inck=16)
            wtc_v[e_] = (f"s_wtc{eb}", V(f"s_wtc{eb}"))

        gth_hist = {}
        vmm_hist = {}
        l1_hist = {}
        tec_v = {}

        def emit_gt_dma(bi):
            e_, tb_ = divmod(bi, NTB)
            for kk in range(4):
                ki = tb_ * 4 + kk
                gb = kk
                gw_ = [idx_v[e_],
                       ("s_xns0", 16 * (NC // 2)), ("s_xns1", 16 * (NC // 2)),
                       ("s_trc", max(0, 4 * bi + kk - 3))]
                P.op("gpsimd", fcl(lambda e, e_=0, ki=0, gb=0, eb=0:
                     e.indirect_dma_start(
                         out=xng[:, gb * 1024:(gb + 1) * 1024],
                         out_offset=None,
                         in_=xn_tm.ap(),
                         in_offset=IndirectOffsetOnAxis(
                             idxs[:, eb * 12 + ki:eb * 12 + ki + 1], 0),
                         oob_is_err=False),
                     e_=e_, ki=ki, gb=gb, eb=e_ % 2),
                    waits=gw_, inc=f"s_gth{kk}", inck=16)
            gth_hist[e_] = [(f"s_gth{kk}", V(f"s_gth{kk}"))
                            for kk in range(4)]

        def emit_gt_pe(bi):
            buf = bi % 2
            for kk in range(4):
                gb = kk
                gthv = (f"s_gth{kk}", 16 * (bi + 1))
                for k in range(8):
                    P.op("tensor", fcl(lambda e, k=0, gb=0: e.transpose(
                        psE[:, k * 128:(k + 1) * 128],
                        xng[:, gb * 1024 + k * 128: gb * 1024 + (k + 1) * 128],
                        identb[:, :]), k=k, gb=gb),
                        waits=([gthv, ("s_tec", V("s_tec")),
                                ("s_tev2", V("s_tev2")), ("s_tev0", NC)]
                               if k == 0 else ()),
                        inc=("s_trc" if k == 7 else None))
                P.op("vector", fcl(lambda e, buf=0, kk=0: e.tensor_copy(
                    xnTc[:, buf * 4096:(buf + 1) * 4096].rearrange(
                        "p (s t) -> p s t", s=8)[:, :, kk * 128:(kk + 1) * 128],
                    psE[:, :].rearrange("p (s t) -> p s t", t=128)),
                    buf=buf, kk=kk),
                    waits=[("s_trc", V("s_trc"))] +
                          ([("s_l1", l1_hist.get(bi - 2, 0))] if kk == 0 else []),
                    inc="s_tec")
            tec_v[bi] = V("s_tec")

        load_idx(0)
        load_idx(1)

        rot = [0, 1, 2, 3, 4]
        bank_ev = {0: [("s_mask", V("s_mask"))],
                   1: [("s_mask", V("s_mask"))],
                   2: [("s_p3d", V("s_p3d")), ("s_p3t", V("s_p3t")),
                       ("s_p3t2", V("s_p3t2"))],
                   3: [("s_div", V("s_div")), ("s_p3d", V("s_p3d"))],
                   4: [("s_acc", V("s_acc"))]}
        vfree = [[("s_acc", V("s_acc")), ("s_stadd", V("s_stadd"))],
                 [("s_stadd", V("s_stadd"))]]
        ri = 0
        vmm_pending = []
        h1_free = list(pcd)
        h2_free = list(pcd)
        ybuf_free = {0: list(pcd), 1: list(pcd)}

        emit_gt_dma(0)
        emit_gt_dma(1)
        emit_gt_pe(0)
        for ei in range(NE):
            s1 = slots[wslot(ei, 0)]
            s2 = slots[wslot(ei, 1)]
            s3 = slots[wslot(ei, 2)]
            for half in range(2):
                bb = rot[ri % 5]
                ri += 1
                P.op("tensor", fcl(lambda e, bb=0, half=0: e.matmul(
                    ps[bb][:, :], onesrow[:, :],
                    b3s[:, half * 512:half * 512 + 512],
                    start=True, stop=True), bb=bb, half=half),
                    waits=[("s_bl3", 16 * (ei + 1)), ("gi", 5)] + bank_ev[bb],
                    inc="s_b3m")
                P.op("vector", fcl(lambda e, bb=0, half=0: e.tensor_copy(
                    b3bc[:, half * 512:half * 512 + 512], ps[bb][:, :]),
                    bb=bb, half=half),
                    waits=[("s_b3m", V("s_b3m"))] +
                          ([("s_e3", V("s_e3"))] if ei > 0 else []),
                    inc="s_b3c")
                bank_ev[bb] = [("s_b3c", V("s_b3c"))]
            b3v = V("s_b3c")
            for tb in range(NTB):
                bi = ei * NTB + tb
                buf = bi % 2
                xoff = buf * 4096
                if bi + 2 < NE * NTB:
                    emit_gt_dma(bi + 2)
                for ot in range(8):
                    b = rot[ri % 5]
                    ri += 1
                    for s in range(8):
                        P.op("tensor", fcl(lambda e, s=0, b=0, ot=0, s1=None,
                                           xoff=0: e.matmul(
                             ps[b][:, :],
                             s1[:, s * 1024 + ot * 128: s * 1024 + ot * 128 + 128],
                             xnTc[:, xoff + s * 512: xoff + (s + 1) * 512],
                             start=(s == 0), stop=(s == 7)),
                             s=s, b=b, ot=ot, s1=s1, xoff=xoff),
                            waits=(([wready[(ei, 0)], ("s_tec", tec_v[bi])]
                                    + bank_ev[b])
                                   if s == 0 else ()),
                            inc=("s_l1" if s == 7 else None))
                    P.op("scalar", fcl(lambda e, b=0, ot=0, ei=0: e.activation(
                        h1T[:, ot * 512:(ot + 1) * 512], ps[b][:, :], AF.Relu,
                        bias=b12s[:, ei * 8 + ot:ei * 8 + ot + 1]),
                        b=b, ot=ot, ei=ei),
                        waits=[("s_l1", V("s_l1")), ("s_bld", 16)] + h1_free,
                        inc="s_e1")
                    bank_ev[b] = [("s_e1", V("s_e1"))]
                h1v = V("s_e1")
                l1_hist[bi] = V("s_l1")
                if bi + 1 < NE * NTB:
                    emit_gt_pe(bi + 1)
                for ot in range(8):
                    b = rot[ri % 5]
                    ri += 1
                    for s in range(8):
                        P.op("tensor", fcl(lambda e, s=0, b=0, ot=0, s2=None:
                             e.matmul(ps[b][:, :],
                                      s2[:, s * 1024 + ot * 128: s * 1024 + ot * 128 + 128],
                                      h1T[:, s * 512:(s + 1) * 512],
                                      start=(s == 0), stop=(s == 7)),
                             s=s, b=b, ot=ot, s2=s2),
                            waits=(([("s_e1", h1v), wready[(ei, 1)]]
                                    + bank_ev[b]) if s == 0 else ()),
                            inc=("s_l2" if s == 7 else None))
                    P.op("scalar", fcl(lambda e, b=0, ot=0, ei=0: e.activation(
                        h2T[:, ot * 512:(ot + 1) * 512], ps[b][:, :], AF.Relu,
                        bias=b12s[:, 64 + ei * 8 + ot:64 + ei * 8 + ot + 1]),
                        b=b, ot=ot, ei=ei),
                        waits=[("s_l2", V("s_l2"))] + h2_free, inc="s_e2")
                    bank_ev[b] = [("s_e2", V("s_e2"))]
                h2v = V("s_e2")
                h1_free = [("s_l2", V("s_l2"))]
                for t4 in range(4):
                    for half in range(2):
                        b = rot[ri % 5]
                        ri += 1
                        for s in range(8):
                            P.op("tensor", fcl(lambda e, s=0, b=0, t4=0, half=0,
                                               s3=None: e.matmul(
                                 ps[b][:, :],
                                 h2T[:, s * 512 + t4 * 128: s * 512 + t4 * 128 + 128],
                                 s3[:, s * 1024 + half * 512: s * 1024 + half * 512 + 512],
                                 start=(s == 0), stop=(s == 7)),
                                 s=s, b=b, t4=t4, half=half, s3=s3),
                                waits=(([("s_e2", h2v),
                                         wready[(ei, 2)]] + bank_ev[b])
                                       if s == 0 else ()),
                                inc=("s_l3" if s == 7 else None))
                        P.op("vector", fcl(lambda e, b=0, half=0:
                             e.tensor_tensor(
                                 ybuf[:, half * 512:half * 512 + 512],
                                 ps[b][:, :],
                                 b3bc[:, half * 512:half * 512 + 512], ALU.add),
                             b=b, half=half),
                            waits=[("s_l3", V("s_l3")), ("s_b3c", b3v)]
                                  + ybuf_free[half],
                            inc="s_ya")
                        P.op("scalar", fcl(lambda e, half=0: e.activation(
                            ybuf[:, half * 512:half * 512 + 512],
                            ybuf[:, half * 512:half * 512 + 512],
                            AF.Relu), half=half),
                            waits=[("s_ya", V("s_ya"))],
                            inc="s_e3")
                        bank_ev[b] = [("s_e3", V("s_e3"))]
                        first = (tb == 0 and t4 == 0)
                        last = (tb == NTB - 1 and t4 == 3)
                        vw = [("s_e3", V("s_e3")), wtc_v[ei]]
                        if first:
                            vw += [("s_vev", V("s_vev"))] + vfree[half]
                        vmm_pending.append(
                            (fcl(lambda e, vb=0, half=0, ki=0, eb=0,
                                 first=False, last=False: e.matmul(
                                     ps[vb][0:1, 0:512],
                                     wtcs[:, eb * 12 + ki:eb * 12 + ki + 1],
                                     ybuf[:, half * 512:half * 512 + 512],
                                     start=first, stop=last),
                                 vb=5 + half, half=half, ki=tb * 4 + t4,
                                 eb=ei % 2, first=first, last=last),
                             vw, half))
                        if len(vmm_pending) > 1:
                            fn0, vw0, half0 = vmm_pending.pop(0)
                            P.op("tensor", fn0, waits=vw0, inc="s_vmm")
                            ybuf_free[half0] = [("s_vmm", V("s_vmm"))]
                h2_free = [("s_l3", V("s_l3"))]
            while vmm_pending:
                fn0, vw0, half0 = vmm_pending.pop(0)
                P.op("tensor", fn0, waits=vw0, inc="s_vmm")
                ybuf_free[half0] = [("s_vmm", V("s_vmm"))]
            vmm_hist[ei] = V("s_vmm")
            vmmv = V("s_vmm")
            P.op("vector", lambda e: e.tensor_copy(vstg[:, 0:512], ps[5][0:1, 0:512]),
                 waits=[("s_vmm", vmmv), ("s_vst", 16 * ei)])
            P.op("vector", lambda e: e.tensor_copy(vstg[:, 512:1024],
                                                   ps[6][0:1, 0:512]),
                 inc="s_vev")
            P.op("sync", fcl(lambda e, ei=0: e.dma_start(
                out=v_in[ei:ei + 1, :], in_=vstg[:, :]), ei=ei),
                waits=[("s_vev", V("s_vev"))], inc="s_vst", inck=16)
            if ei == 5:
                P.op("gpsimd", lambda e: e.collective_compute(
                    "AllReduce", ALU.add,
                    replica_groups=[[0, 1, 2, 3, 4, 5, 6, 7]],
                    ins=[v_in[0:6, :].opt()], outs=[v_out[0:6, :].opt()]),
                    waits=[("s_vst", 16 * 6)], inc="s_cc2")
                P.op("sync", lambda e: e.dma_start(
                    out=v32[0:6, :], in_=v_out[0:6, :]),
                    waits=[("s_cc2", 1)], inc="s_vld", inck=16)
            if ei == 6:
                P.op("gpsimd", lambda e: e.collective_compute(
                    "AllReduce", ALU.add,
                    replica_groups=[[0, 1, 2, 3, 4, 5, 6, 7]],
                    ins=[v_in[6:7, :].opt()], outs=[v_out[6:7, :].opt()]),
                    waits=[("s_vst", 16 * 7)], inc="s_cc2")
                P.op("sync", lambda e: e.dma_start(
                    out=v32[6:7, :], in_=v_out[6:7, :]),
                    waits=[("s_cc2", 2)], inc="s_vld", inck=16)
            vfree = [[], []]
            slot_free[wslot(ei, 0)] = [("s_l1", V("s_l1"))]
            slot_free[wslot(ei, 1)] = [("s_l2", V("s_l2"))]
            slot_free[wslot(ei, 2)] = [("s_l3", V("s_l3"))]
            if ei + 1 < NE:
                load_b3(ei + 1, waits=[("s_b3m", V("s_b3m"))])
                if (ei + 1, 1) not in wready:
                    load_w(ei + 1, 1, w2b_d)
                load_w(ei + 1, 2, w3b_d)
            if ei + 2 < NE:
                load_w(ei + 2, 0, w1b_d)
                load_idx(ei + 2)

        RQ = ("gpsimd", "sync", "scalar")
        ald_v = {}

        def emit_accld(t):
            eb = t % 4
            rr = t % 3
            P.op(RQ[rr], fcl(lambda e, t=0, eb=0: e.dma_start(
                out=accld[:, eb * 1024:(eb + 1) * 1024], in_=acc_scr[t]),
                t=t, eb=eb),
                waits=[("s_o5", 2 * t - 6), ("s_acst", 16 * NC)] +
                      (pcd if t < 4 else []),
                inc=f"s_aldr{rr}", inck=16)
            ald_v[t] = (f"s_aldr{rr}", V(f"s_aldr{rr}"))

        for t in range(4):
            emit_accld(t)
        P.op("gpsimd", lambda e: e.collective_compute(
            "AllReduce", ALU.add,
            replica_groups=[[0, 1, 2, 3, 4, 5, 6, 7]],
            ins=[v_in[7:8, :].opt()], outs=[v_out[7:8, :].opt()]),
            waits=[("s_vst", 16 * NE)], inc="s_cc2")
        P.op("sync", lambda e: e.dma_start(out=v32[7:8, :], in_=v_out[7:8, :]),
             waits=[("s_cc2", 3)], inc="s_vld", inck=16)
        P.op("vector", lambda e: e.tensor_copy(vsb[:, :], v32[:, :]),
            waits=[("s_vld", 48)], inc="s_vsb")
        P.emit(nc)

        if stop_phase < 5:
            sDE.close()
            return nc
        P.emit_marker = None
        p4d = [("s_l1", V("s_l1")), ("s_l2", V("s_l2")), ("s_l3", V("s_l3")),
               ("s_e1", V("s_e1")), ("s_e2", V("s_e2")), ("s_e3", V("s_e3")),
               ("s_vmm", V("s_vmm")), ("s_vev", V("s_vev")), ("s_vsb", 1),
               ("s_vld", 48), ("s_bld", 16), ("s_bl3", V("s_bl3"))] +               [(f"s_wl{i}", V(f"s_wl{i}")) for i in range(5)]
        P = Prog(book)
        ost_v = {}
        for t in range(NC):
            eb = t % 4
            rr = t % 3
            for half in range(2):
                b = (t % 2) * 2 + half
                mw = [("s_vsb", 1), ("s_p3d", V("s_p3d")), ("s_o5", 2 * t - 2)]
                if t < 2:
                    mw += bank_ev[b]
                P.op("tensor", fcl(lambda e, t=0, half=0, b=0: e.matmul(
                    ps[b][:, :], indT[:, t * 128:(t + 1) * 128],
                    vsb[:, half * 512:(half + 1) * 512],
                    start=True, stop=True), t=t, half=half, b=b),
                    waits=mw, inc="s_m5")
                ow = [("s_m5", V("s_m5")), ald_v[t]]
                if t >= 4:
                    ow.append(ost_v[t - 4])
                if t == 0:
                    ow += p4d
                P.op("vector", fcl(lambda e, eb=0, half=0, b=0: e.tensor_tensor(
                    outst[:, eb * 1024 + half * 512: eb * 1024 + half * 512 + 512],
                    ps[b][:, :],
                    accld[:, eb * 1024 + half * 512: eb * 1024 + half * 512 + 512],
                    ALU.add), eb=eb, half=half, b=b),
                    waits=ow, inc="s_o5")
            P.op(RQ[rr], fcl(lambda e, t=0, eb=0: e.dma_start(
                out=out_d[t * 128:(t + 1) * 128, :],
                in_=outst[:, eb * 1024:(eb + 1) * 1024]), t=t, eb=eb),
                waits=[("s_o5", V("s_o5"))], inc=f"s_ostr{rr}", inck=16)
            ost_v[t] = (f"s_ostr{rr}", V(f"s_ostr{rr}"))
            if t + 4 < NC:
                emit_accld(t + 4)
        for rr in range(3):
            P.op("gpsimd", fcl(lambda e, rr=0: e.wait_ge(
                book.sem(f"s_ostr{rr}"), book.v(f"s_ostr{rr}")), rr=rr),
                bare=True)
        P.emit(nc)
        sDE.close()

    return nc


_NC_CACHE = {}


def kernel(**inputs):
    x = np.asarray(inputs["L0_attn_x"], dtype=np.float32)
    wqkv = np.asarray(inputs["L0_attn_W_qkv"], dtype=np.float32)
    wo = np.ascontiguousarray(np.asarray(inputs["L0_attn_W_out"], dtype=np.float32))
    gw = np.ascontiguousarray(np.asarray(inputs["L0_ffn_gate_w"], dtype=np.float32))
    gb = np.ascontiguousarray(
        np.asarray(inputs["L0_ffn_gate_b"], dtype=np.float32).reshape(8, 1))
    w1 = np.ascontiguousarray(np.asarray(inputs["L0_ffn_w1"], dtype=np.float32))
    w2 = np.ascontiguousarray(np.asarray(inputs["L0_ffn_w2"], dtype=np.float32))
    w3 = np.ascontiguousarray(np.asarray(inputs["L0_ffn_w3"], dtype=np.float32))
    b1 = np.asarray(inputs["L0_ffn_b1"], dtype=np.float32)
    b2 = np.asarray(inputs["L0_ffn_b2"], dtype=np.float32)
    b3 = np.ascontiguousarray(
        np.asarray(inputs["L0_ffn_b3"], dtype=np.float32).reshape(1, 8192))

    wq = np.ascontiguousarray(wqkv[:, 0:1024])
    wk = np.ascontiguousarray(wqkv[:, 1024:2048])
    wv = np.ascontiguousarray(wqkv[:, 2048:3072])
    b12 = np.ascontiguousarray(
        np.stack([b1, b2]).reshape(2, 8, 8, 128).transpose(3, 0, 1, 2).reshape(128, 128))

    if "nc" not in _NC_CACHE:
        _NC_CACHE["nc"] = build_nc()
    nc = _NC_CACHE["nc"]

    in_maps = []
    for c in range(8):
        beta, half = c >> 1, c & 1
        xs = np.ascontiguousarray(x[beta, half * NT:(half + 1) * NT, :])
        in_maps.append({
            "x": xs, "wq": wq, "wk": wk, "wv": wv, "wo": wo,
            "gw": gw, "gb": gb, "w1": w1, "w2": w2, "w3": w3,
            "b12": b12, "b3": b3,
            "psc": np.array([[float(half)]], dtype=np.float32),
        })

    res = run_bass_kernel_spmd(nc, in_maps, core_ids=list(range(8)))
    out = np.empty((4, 8192, 1024), dtype=np.float32)
    for c in range(8):
        beta, half = c >> 1, c & 1
        out[beta, half * NT:(half + 1) * NT, :] = res.results[c]["out"]
    return out


# version 50 (final)
# speedup vs baseline: 1.0197x; 1.0197x over previous
import numpy as np
from contextlib import ExitStack

import concourse.bass as bass
import concourse.mybir as mybir
from concourse.bass_utils import run_bass_kernel_spmd
from concourse.bass_core_rust import IndirectOffsetOnAxis

F32 = mybir.dt.float32
BF16 = mybir.dt.bfloat16
I32 = mybir.dt.int32
AF = mybir.ActivationFunctionType
ALU = mybir.AluOpType
AX = mybir.AxisListType.X

NT = 4096
D = 1024
NH = 16
NC = 32
NG = 8
NE = 8
EPS_ATTN = 1e-6
EPS_LN = 1e-5
KCAP = 1536
NTB = 3
BIGOOB = 1.0e9

ENGINES = ("sync", "gpsimd", "scalar", "tensor", "vector")


_AT = ("t_ps", "s_tr0", "s_kvmm", "s_stA", "s_lgmm", "s_at", "s_nmm",
       "s_st2", "s_qmm", "s_tr2", "s_wout", "s_m5", "s_l1", "s_l2", "s_l3",
       "s_vmm", "s_b3m", "s_trc", "s_pfm", "s_rmm", "s_p3t", "s_p3t2")
_AV = ("v_ps", "vi", "s_zl", "s_sini", "s_stcp", "s_rZ", "s_kvev", "s_soadd",
       "s_mask", "s_stadd", "s_acc", "s_o5", "s_tev0", "s_tec", "s_ya",
       "s_b3c", "s_vev", "s_vsb", "s_rq", "s_zm", "s_tid", "s_ohb", "s_psb",
       "s_rsb", "s_p3a", "s_p3b", "s_p3w", "s_p3v", "s_p3d", "s_nmr",
       "s_lns", "s_lnm", "s_phf")
_AS = ("s_sq", "s_lnl", "s_rstd", "s_lnf", "s_phE", "s_lgE", "s_qphE",
       "s_e1", "s_e2", "s_e3", "s_div", "s_tev2")
ALIAS = {}
for n_ in _AT:
    ALIAS[n_] = "a_tensor"
for n_ in _AV:
    ALIAS[n_] = "a_vector"
for n_ in _AS:
    ALIAS[n_] = "a_scalar"


class Book:

    def __init__(self, nc, stack):
        self.nc = nc
        self.stack = stack
        self.h = {}
        self.n = {}
        self.lg = {}
        self.amap = {}

    def sem(self, name):
        name = ALIAS.get(name, name)
        if name not in self.h:
            self.h[name] = self.stack.enter_context(self.nc.semaphore(name))
            self.n[name] = 0
        return self.h[name]

    def v(self, name):
        return self.lg.get(name, 0)

    def abs_of(self, name, v):
        if name not in ALIAS:
            return name, v
        m = self.amap.get(name, {})
        if v not in m:
            raise KeyError(f"wait on {name}>={v} never recorded "
                           f"(logical count {self.lg.get(name, 0)})")
        return ALIAS[name], m[v]


class Prog:

    last_evt = {}

    def __init__(self, book):
        self.book = book
        self.ops = {e: [] for e in ENGINES}

    def op(self, eng, fn, waits=(), inc=None, inck=1, bare=False):
        wl = [("L", sn, v) for sn, v in waits]
        chain = (eng in ("vector", "scalar", "gpsimd")
                 and inck != 16 and not bare)
        if chain and eng in Prog.last_evt:
            wl.append(Prog.last_evt[eng])
        if inc is None and chain:
            inc = "a_" + eng
        incv = None
        if inc is not None:
            phys = ALIAS.get(inc, inc)
            self.book.sem(inc)
            self.book.n[phys] += inck
            self.book.lg[inc] = self.book.lg.get(inc, 0) + inck
            self.book.amap.setdefault(inc, {})[self.book.lg[inc]] =                 self.book.n[phys]
            incv = (phys, inck)
            if chain:
                Prog.last_evt[eng] = ("A", phys, self.book.n[phys])
        for _, sn, _v in wl:
            self.book.sem(sn if _ == "L" else sn)
        self.ops[eng].append((tuple(wl), fn, incv))
        return self.book.lg[inc] if inc else None

    def emit(self, nc):
        book = self.book

        def make(lst):
            resolved = []
            for waits, fn, incv in lst:
                rw = []
                for kind, sn, v in waits:
                    if v <= 0:
                        continue
                    if kind == "L":
                        rw.append(book.abs_of(sn, v))
                    else:
                        rw.append((sn, v))
                resolved.append((rw, fn, incv))

            def body(handle):
                last = {}
                for waits, fn, incv in resolved:
                    for s, v in waits:
                        if last.get(s, -1) >= v:
                            continue
                        handle.wait_ge(book.h[s], v)
                        last[s] = v
                    ins = fn(handle)
                    if incv is not None:
                        assert ins is not None
                        ins.then_inc(book.h[incv[0]], incv[1])
            return body

        with nc.Block() as blk:
            for eng in ENGINES:
                if self.ops[eng]:
                    getattr(blk, eng)(make(self.ops[eng]))


def fcl(f, **kw):
    return lambda eng, _f=f, _k=kw: _f(eng, **_k)


def build_nc(debug=False, stop_phase=5, a_chunks=NC, a_coll=True, a_parts='full', p3n=4):
    nc = bass.Bass("TRN2", target_bir_lowering=False)

    x_d = nc.dram_tensor("x", [NT, D], F32, kind="ExternalInput")
    wq_d = nc.dram_tensor("wq", [D, D], F32, kind="ExternalInput")
    wk_d = nc.dram_tensor("wk", [D, D], F32, kind="ExternalInput")
    wv_d = nc.dram_tensor("wv", [D, D], F32, kind="ExternalInput")
    wo_d = nc.dram_tensor("wo", [D, D], F32, kind="ExternalInput")
    gw_d = nc.dram_tensor("gw", [D, NE], F32, kind="ExternalInput")
    gb_d = nc.dram_tensor("gb", [NE, 1], F32, kind="ExternalInput")
    w1_d = nc.dram_tensor("w1", [NE, D, D], F32, kind="ExternalInput")
    w2_d = nc.dram_tensor("w2", [NE, D, D], F32, kind="ExternalInput")
    w3_d = nc.dram_tensor("w3", [NE, D, D], F32, kind="ExternalInput")
    b12_d = nc.dram_tensor("b12", [128, 128], F32, kind="ExternalInput")
    b3_d = nc.dram_tensor("b3", [1, NE * D], F32, kind="ExternalInput")
    psc_d = nc.dram_tensor("psc", [1, 1], F32, kind="ExternalInput")
    out_d = nc.dram_tensor("out", [NT, D], F32, kind="ExternalOutput")

    dbgk = "ExternalOutput" if debug else "Internal"
    kv_scr = nc.dram_tensor("kv_scr", [NC, 128, 2080], BF16, kind="Internal")
    acc_scr = nc.dram_tensor("acc_scr", [NC, 128, D], F32, kind=dbgk)
    exp_scr = nc.dram_tensor("exp_scr", [NE, NT], F32, kind=dbgk)
    ag_in = nc.dram_tensor("ag_in", [1, 66568], F32, kind="Internal")
    ag_out = nc.dram_tensor("ag_out", [2, 66568], F32, kind="Internal")
    v_in = nc.dram_tensor("v_in", [NE, D], F32, kind="Internal")
    v_out = nc.dram_tensor("v_out", [NE, D], F32, kind=dbgk)
    xn_tm = nc.dram_tensor("xn_tm", [NT, D], BF16, kind="Internal")
    w1b_d = nc.dram_tensor("w1b", [NE, D, D], BF16, kind="Internal")
    w2b_d = nc.dram_tensor("w2b", [NE, D, D], BF16, kind="Internal")
    w3b_d = nc.dram_tensor("w3b", [NE, D, D], BF16, kind="Internal")
    tok_scr = nc.dram_tensor("tok_scr", [NE, KCAP], I32, kind=dbgk)
    wtc_scr = nc.dram_tensor("wtc_scr", [NE, KCAP], BF16, kind=dbgk)
    xn_dbg = nc.dram_tensor("xn_dbg", [128, 8 * NT], BF16,
                            kind=dbgk) if debug else None

    stack = ExitStack()
    with stack:
        book = Book(nc, stack)
        V = book.v
        S = stack.enter_context

        ps = [S(nc.psum_tensor(f"ps{i}", [128, 512], F32)) for i in range(7)]
        psE = S(nc.psum_tensor("psE", [128, 1024], BF16))

        wpool = [S(nc.sbuf_tensor(f"wp{i}", [128, 8192], BF16)) for i in range(4)]
        gws = S(nc.sbuf_tensor("gws", [128, 64], BF16))
        gbs = S(nc.sbuf_tensor("gbs", [NE, 1], F32))
        identb = S(nc.sbuf_tensor("identb", [128, 128], BF16))
        identf = S(nc.sbuf_tensor("identf", [128, 128], F32))
        masktri = S(nc.sbuf_tensor("masktri", [128, 128], F32))
        onesrow = S(nc.sbuf_tensor("onesrow", [1, 128], BF16))
        psc128 = S(nc.sbuf_tensor("psc128", [128, 1], F32))
        pscb = S(nc.sbuf_tensor("pscb", [1, 1], BF16))
        triuSb = S(nc.sbuf_tensor("triuSb", [128, 128], BF16))
        onesM = S(nc.sbuf_tensor("onesM", [128, 128], BF16))
        iotaEr = S(nc.sbuf_tensor("iotaEr", [128, 8], F32))
        iotac = S(nc.sbuf_tensor("iotac", [128, 1], I32))
        p3c = S(nc.sbuf_tensor("p3c", [128, 48], F32))
        wt1b = S(nc.sbuf_tensor("wt1b", [128, 4], BF16))
        p3i = S(nc.sbuf_tensor("p3i", [128, 8], I32))
        posi = S(nc.sbuf_tensor("posi", [128, 4], I32))
        tid = S(nc.sbuf_tensor("tid", [128, 2], I32))
        indT = S(nc.sbuf_tensor("indT", [NE, NT], BF16))
        rp = [S(nc.sbuf_tensor(f"rp{i}", [128, 520], F32)) for i in range(2)]
        S_runb = S(nc.sbuf_tensor("S_runb", [128, 1040], BF16))
        zparts = S(nc.sbuf_tensor("zparts", [NE, 10], F32))
        rZ = S(nc.sbuf_tensor("rZ", [NE, 1], F32))
        zb2 = S(nc.sbuf_tensor("zb2", [NE, 2], F32))
        lnst = S(nc.sbuf_tensor("lnst", [128, 16], F32))
        dnm4 = S(nc.sbuf_tensor("dnm4", [128, 4], F32))
        rec4 = S(nc.sbuf_tensor("rec4", [128, 4], F32))
        p3b = S(nc.sbuf_tensor("p3b", [128, 224], F32))
        iota32 = S(nc.sbuf_tensor("iota32", [128, 32], F32))
        expst = S(nc.sbuf_tensor("expst", [NE, 512], F32))
        expld = S(nc.sbuf_tensor("expld", [NE, 512], F32))
        prT = S(nc.sbuf_tensor("prT", [NE, 512], F32))
        ohsb = S(nc.sbuf_tensor("ohsb", [128, 32], BF16))
        vsb = S(nc.sbuf_tensor("vsb", [NE, D], BF16))

        P = Prog(book)
        P.op("gpsimd", lambda e: e.iota(identb[:, :], [[1, 128]], channel_multiplier=-1,
                                        allow_small_or_imprecise_dtypes=True), inc="gi")
        P.op("gpsimd", lambda e: e.iota(identf[:, :], [[1, 128]],
                                        channel_multiplier=-1,
                                        allow_small_or_imprecise_dtypes=True), inc="gi")
        P.op("gpsimd", lambda e: e.iota(masktri[:, :], [[1, 128]],
            channel_multiplier=-1,
            allow_small_or_imprecise_dtypes=True), inc="gi")
        P.op("gpsimd", lambda e: e.iota(iota32[:, :].rearrange(
            "p (k o) -> p k o", o=8), [[0, 4], [1, 8]], base=-8,
            channel_multiplier=0,
            allow_small_or_imprecise_dtypes=True), inc="gi")
        P.op("gpsimd", lambda e: e.memset(onesrow[:, :], 1.0), inc="gi")
        P.op("gpsimd", lambda e: e.iota(triuSb[:, :], [[1, 128]],
                                        channel_multiplier=-1,
                                        allow_small_or_imprecise_dtypes=True),
             inc="gi")
        P.op("gpsimd", lambda e: e.memset(onesM[:, :], 1.0), inc="gi")
        P.op("gpsimd", lambda e: e.iota(iotaEr[:, :], [[1, 8]],
                                        channel_multiplier=0,
                                        allow_small_or_imprecise_dtypes=True),
             inc="gi")
        P.op("gpsimd", lambda e: e.iota(iotac[:, :], [[0, 1]],
                                        channel_multiplier=1,
                                        allow_small_or_imprecise_dtypes=True),
             inc="gi")
        P.op("vector", lambda e: e.tensor_single_scalar(
            triuSb[:, :], triuSb[:, :], 0.0, ALU.is_gt),
            waits=[("gi", 6)], inc="vi")
        P.op("vector", lambda e: e.tensor_single_scalar(
            identb[:, :], identb[:, :], 0.0, ALU.is_equal),
            waits=[("gi", 1)], inc="vi")
        P.op("vector", lambda e: e.tensor_single_scalar(
            identf[:, :], identf[:, :], 0.0, ALU.is_equal),
            waits=[("gi", 2)], inc="vi")
        P.op("vector", lambda e: e.tensor_single_scalar(
            masktri[:, :], masktri[:, :], 0.0, ALU.is_ge),
            waits=[("gi", 3)], inc="vi")

        def wload8(sname, dst, src, ncols=1024, waits=()):
            for s in range(8):
                P.op("gpsimd", fcl(lambda e, s=0, dst=None, src=None, ncols=0:
                                   e.dma_start(
                                       out=dst[:, s * ncols:(s + 1) * ncols],
                                       in_=src[s * 128:(s + 1) * 128, :]),
                                   s=s, dst=dst, src=src, ncols=ncols),
                     waits=(waits if s == 0 else ()), inc=sname, inck=16)

        wload8("w_qk", wpool[2], wk_d)
        wload8("w_kv", wpool[3], wv_d)
        wload8("w_qk", wpool[1], wq_d)
        wload8("w_gw", gws, gw_d, ncols=8)
        P.op("gpsimd", lambda e: e.dma_start(out=gbs[:, :], in_=gb_d[:, :]),
             inc="w_gw", inck=16)
        P.op("gpsimd", lambda e: e.dma_start(out=pscb[:, :], in_=psc_d[:, :]),
             inc="w_ps", inck=16)
        P.op("tensor", lambda e: e.matmul(ps[0][:, 0:1], onesrow[:, :], pscb[:, :],
                                          start=True, stop=True),
             waits=[("w_ps", 16), ("gi", 5)], inc="t_ps")
        P.op("vector", lambda e: e.tensor_copy(psc128[:, :], ps[0][:, 0:1]),
             waits=[("t_ps", 1)], inc="v_ps")
        P.emit(nc)

        sAC = ExitStack()
        A = sAC.enter_context
        xnT = A(nc.sbuf_tensor("xnT", [128, 8 * NT], BF16))
        wos = A(nc.sbuf_tensor("wos", [128, 8192], BF16))
        pkv = [A(nc.sbuf_tensor(f"pkv{i}", [128, 2080], BF16)) for i in range(2)]
        xt = [A(nc.sbuf_tensor(f"xt{i}", [128, D], F32)) for i in range(3)]
        accst = A(nc.sbuf_tensor("accst", [128, D], F32))
        pre = A(nc.sbuf_tensor("pre", [128, D], BF16))
        preT = A(nc.sbuf_tensor("preT", [128, D], BF16))
        xnt = A(nc.sbuf_tensor("xnt", [128, 2 * D], BF16))
        atm = A(nc.sbuf_tensor("atm", [128, 512], BF16))
        S_own = [A(nc.sbuf_tensor(f"So{i}", [128, 520], F32)) for i in range(2)]
        zi = A(nc.sbuf_tensor("zi", [8, 128], I32))

        qkT = wpool[0]
        wqs, wks, wvs = wpool[1], wpool[2], wpool[3]
        tmpe, tmpt = pre, preT

        P = Prog(book)
        P.op("gpsimd", lambda e: e.memset(
            pkv[0][:, 1024:2064].rearrange("p (h o) -> p h o", o=65)[:, :, 64:65],
            1.0), inc="gones")
        P.op("gpsimd", lambda e: e.memset(
            pkv[1][:, 1024:2064].rearrange("p (h o) -> p h o", o=65)[:, :, 64:65],
            1.0), inc="gones")
        P.op("vector", lambda e: e.memset(zi[:, :], 0), inc="s_zm")
        for kk in range(12):
            P.op("gpsimd", fcl(lambda e, kk=0: e.dma_start(
                out=tok_scr[:, kk * 128:(kk + 1) * 128], in_=zi[:, :]), kk=kk),
                waits=[("s_zm", 1)], inc="s_zf", inck=16)
            P.op("gpsimd", fcl(lambda e, kk=0: e.dma_start(
                out=wtc_scr[:, kk * 128:(kk + 1) * 128], in_=zi[:, :]), kk=kk),
                waits=[("s_zm", 1)], inc="s_zf", inck=16)

        KV_EVEN = [0, 1, 2, 3]
        KV_ODD = [4, 5, 6, 3]
        bank_free = {i: ([("v_ps", 1)] if i == 0 else []) for i in range(7)}
        ln_done = {}
        sq_v = {}
        lnf_v = {}

        for c in range(a_chunks):
            xb = c % 2
            x3 = c % 3
            B = xb * 8
            xw = []
            if c >= 3:
                lf, sq = ln_done[c - 3]
                xw = [("s_lnf", lf), ("s_sq", sq)]
            P.op("sync", fcl(lambda e, c=0, x3=0: e.dma_start(
                out=xt[x3][:, :], in_=x_d[c * 128:(c + 1) * 128, :]), c=c, x3=x3),
                waits=xw, inc=f"s_xt{x3}", inck=16)
            P.op("vector", fcl(lambda e, x3=0, B=0: e.reduce_sum(
                lnst[:, B:B + 1], xt[x3][:, :], axis=AX), x3=x3, B=B),
                waits=[(f"s_xt{x3}", V(f"s_xt{x3}"))])
            P.op("scalar", fcl(lambda e, x3=0, xb=0, B=0: e.activation(
                xnt[:, xb * D:(xb + 1) * D], xt[x3][:, :], AF.Square,
                accum_out=lnst[:, B + 1:B + 2]), x3=x3, xb=xb, B=B),
                waits=[(f"s_xt{x3}", V(f"s_xt{x3}")), ("s_tr0", max(0, c - 1)),
                       (f"s_xns{xb}", 16 * (c // 2))],
                inc="s_sq")
            sq_v[c] = V("s_sq")
            P.op("vector", fcl(lambda e, B=0: e.tensor_single_scalar(
                lnst[:, B + 2:B + 3], lnst[:, B:B + 1], -1.0 / D, ALU.mult), B=B),
                waits=[("s_sq", sq_v[c])])
            P.op("vector", fcl(lambda e, B=0: e.tensor_tensor(
                lnst[:, B + 3:B + 4], lnst[:, B + 2:B + 3], lnst[:, B + 2:B + 3],
                ALU.mult), B=B))
            P.op("vector", fcl(lambda e, B=0: e.tensor_scalar(
                lnst[:, B + 4:B + 5], lnst[:, B + 1:B + 2], 1.0 / D, EPS_LN,
                ALU.mult, ALU.add), B=B))
            P.op("vector", fcl(lambda e, B=0: e.tensor_tensor(
                lnst[:, B + 5:B + 6], lnst[:, B + 4:B + 5], lnst[:, B + 3:B + 4],
                ALU.subtract), B=B), inc="s_lns")
            lnsv = V("s_lns")
            P.op("scalar", fcl(lambda e, B=0: e.activation(
                lnst[:, B + 6:B + 7], lnst[:, B + 5:B + 6], AF.Ln), B=B),
                waits=[("s_lns", lnsv)], inc="s_lnl")
            P.op("vector", fcl(lambda e, B=0: e.tensor_single_scalar(
                lnst[:, B + 3:B + 4], lnst[:, B + 6:B + 7], -0.5, ALU.mult), B=B),
                waits=[("s_lnl", V("s_lnl"))], inc="s_lnm")
            P.op("scalar", fcl(lambda e, B=0: e.activation(
                lnst[:, B + 7:B + 8], lnst[:, B + 3:B + 4], AF.Exp), B=B),
                waits=[("s_lnm", V("s_lnm"))], inc="s_rstd")
            P.op("vector", fcl(lambda e, B=0: e.tensor_tensor(
                lnst[:, B + 2:B + 3], lnst[:, B + 2:B + 3], lnst[:, B + 7:B + 8],
                ALU.mult), B=B),
                waits=[("s_rstd", V("s_rstd"))], inc="s_nmr")
            P.op("scalar", fcl(lambda e, x3=0, xb=0, B=0: e.activation(
                xnt[:, xb * D:(xb + 1) * D], xt[x3][:, :], AF.Identity,
                bias=lnst[:, B + 2:B + 3], scale=lnst[:, B + 7:B + 8]),
                x3=x3, xb=xb, B=B),
                waits=[("s_nmr", V("s_nmr"))], inc="s_lnf")
            lnf_v[c] = V("s_lnf")
            ln_done[c] = (lnf_v[c], sq_v[c])
            P.op("gpsimd", fcl(lambda e, c=0, xb=0: e.dma_start(
                out=xn_tm[c * 128:(c + 1) * 128, :],
                in_=xnt[:, xb * D:(xb + 1) * D]), c=c, xb=xb),
                waits=[("s_lnf", lnf_v[c])], inc=f"s_xns{xb}", inck=16)
            if 16 <= c < 24:
                P.op("gpsimd", fcl(lambda e, s=0: e.dma_start(
                    out=wos[:, s * 1024:(s + 1) * 1024],
                    in_=wo_d[s * 128:(s + 1) * 128, :]), s=c - 16),
                    inc="w_wo", inck=16)
            for k in range(8):
                P.op("tensor", fcl(lambda e, k=0, xb=0: e.transpose(
                    psE[:, k * 128:(k + 1) * 128],
                    xnt[:, xb * D + k * 128: xb * D + (k + 1) * 128],
                    identb[:, :]), k=k, xb=xb),
                    waits=([("s_lnf", lnf_v[c]), ("s_tev0", c), ("vi", 1)]
                           if k == 0 else ()),
                    inc=("s_tr0" if k == 7 else None))
            if c >= 1:
                P.op("vector", fcl(lambda e, cc=0: e.tensor_copy(
                    xnT[:, :].rearrange("p (s t) -> p s t", s=8)
                    [:, :, cc * 128:(cc + 1) * 128],
                    psE[:, :].rearrange("p (s t) -> p s t", t=128)), cc=c - 1),
                    waits=[("s_tr0", c)], inc="s_tev0")
        P.op("vector", fcl(lambda e, cc=0: e.tensor_copy(
            xnT[:, :].rearrange("p (s t) -> p s t", s=8)
            [:, :, cc * 128:(cc + 1) * 128],
            psE[:, :].rearrange("p (s t) -> p s t", t=128)), cc=a_chunks - 1),
            waits=[("s_tr0", a_chunks)], inc="s_tev0")

        if a_parts != "p0":
            kvmm_v = {}
            phE_v = {}
            kvev_v = {}

            def p1_state(c):
                banks = KV_EVEN if c % 2 == 0 else KV_ODD
                pk = pkv[c % 2]
                for h in range(NH):
                    p = h // 2
                    bb = banks[0] if p < 4 else banks[1]
                    co = (p % 4) * 65
                    sw = []
                    if h == 0:
                        sw = [("s_kvev", kvev_v[c]), ("s_phE", phE_v[c]),
                              ("s_soadd", V("s_soadd"))]
                    P.op("tensor", fcl(lambda e, h=0, bb=0, co=0, pk=None:
                         e.matmul(ps[bb][(h % 2) * 64:(h % 2) * 64 + 64,
                                         co:co + 65],
                                  pk[:, h * 64:(h + 1) * 64],
                                  pk[:, 1024 + h * 65:1024 + h * 65 + 65],
                                  start=True, stop=True),
                         h=h, bb=bb, co=co, pk=pk),
                        waits=sw, inc=("s_stA" if h == 15 else None))
                stv = V("s_stA")
                cur, nxt = S_own[(c + 1) % 2], S_own[c % 2]
                if c == 0:
                    P.op("vector", fcl(lambda e, b0=0: e.tensor_copy(
                        S_own[0][:, 0:260], ps[b0][:, 0:260]), b0=banks[0]),
                        waits=[("s_stA", stv)])
                    P.op("vector", fcl(lambda e, b1=0: e.tensor_copy(
                        S_own[0][:, 260:520], ps[b1][:, 0:260]), b1=banks[1]),
                        inc="s_soadd")
                else:
                    P.op("vector", fcl(lambda e, b0=0, cur=None, nxt=None:
                         e.tensor_tensor(nxt[:, 0:260], cur[:, 0:260],
                                         ps[b0][:, 0:260], ALU.add),
                         b0=banks[0], cur=cur, nxt=nxt),
                        waits=[("s_stA", stv)])
                    P.op("vector", fcl(lambda e, b1=0, cur=None, nxt=None:
                         e.tensor_tensor(nxt[:, 260:520], cur[:, 260:520],
                                         ps[b1][:, 0:260], ALU.add),
                         b1=banks[1], cur=cur, nxt=nxt),
                        inc="s_soadd")
                sov = V("s_soadd")
                P.op("sync", fcl(lambda e, c=0, pk=None: e.dma_start(
                    out=kv_scr[c][:, 0:2064], in_=pk[:, 0:2064]), c=c, pk=pk),
                    waits=[("s_kvev", kvev_v[c])], inc=f"s_pkst{c % 2}",
                    inck=16)
                if c % 4 == 3:
                    g = c // 4
                    lb = (KV_EVEN if c % 2 == 0 else KV_ODD)[2]
                    lw = [("s_kvev", kvev_v[c]), ("w_gw", 144),
                          ("s_lgE", V("s_lgE"))]
                    for sidx in range(8):
                        P.op("tensor", fcl(lambda e, sidx=0, lb=0, g=0: e.matmul(
                            ps[lb][0:8, 0:512],
                            gws[:, sidx * 8:(sidx + 1) * 8],
                            xnT[:, sidx * NT + g * 512: sidx * NT + (g + 1) * 512],
                            start=(sidx == 0), stop=(sidx == 7)),
                            sidx=sidx, lb=lb, g=g),
                            waits=(lw if sidx == 0 else ()),
                            inc=("s_lgmm" if sidx == 7 else None))
                    P.op("scalar", fcl(lambda e, lb=0, g=0: e.activation(
                        expst[:, :], ps[lb][0:8, 0:512], AF.Exp, bias=gbs[:, :],
                        accum_out=zparts[:, g:g + 1]), lb=lb, g=g),
                        waits=[("s_lgmm", V("s_lgmm")), ("s_lgst", 16 * g)],
                        inc="s_lgE")
                    P.op("sync", fcl(lambda e, g=0: e.dma_start(
                        out=exp_scr[0:8, g * 512:(g + 1) * 512], in_=expst[:, :]),
                        g=g),
                        waits=[("s_lgE", V("s_lgE"))], inc="s_lgst", inck=16)
                    bank_free[lb] = bank_free[lb] + [("s_lgE", V("s_lgE"))]
                bank_free[banks[0]] = [("s_phE", phE_v[c] - 1),
                                       ("s_kvev", kvev_v[c]), ("s_soadd", sov)]
                bank_free[banks[1]] = [("s_phE", phE_v[c]),
                                       ("s_kvev", kvev_v[c]), ("s_soadd", sov)]
            for c in range(a_chunks):
                banks = KV_EVEN if c % 2 == 0 else KV_ODD
                pk = pkv[c % 2]
                wsrc = [(wks, 0, "w_qk", 256), (wks, 512, "w_qk", 256),
                        (wvs, 0, "w_kv", 128), (wvs, 512, "w_kv", 128)]
                kvmmv = []
                for ti in range(4):
                    wbuf, coff, wsem, wval = wsrc[ti]
                    bb = banks[ti]
                    fw = [("s_tev0", c + 1), (wsem, wval)] + bank_free[bb]
                    for sidx in range(8):
                        P.op("tensor", fcl(lambda e, sidx=0, bb=0, wbuf=None,
                                           coff=0, c=0: e.matmul(
                             ps[bb][:, :],
                             xnT[:, sidx * NT + c * 128: sidx * NT + (c + 1) * 128],
                             wbuf[:, sidx * 1024 + coff: sidx * 1024 + coff + 512],
                             start=(sidx == 0), stop=(sidx == 7)),
                             sidx=sidx, bb=bb, wbuf=wbuf, coff=coff, c=c),
                            waits=(fw if sidx == 0 else ()),
                            inc=("s_kvmm" if sidx == 7 else None))
                    kvmmv.append(V("s_kvmm"))
                if c >= 1:
                    p1_state(c - 1)
                phEl = []
                for ti in range(2):
                    bb = banks[ti]
                    half = ti * 512
                    ew = [("s_kvmm", kvmmv[ti]),
                          ("s_kvev", max(0, 4 * (c - 1) + ti + 1))]
                    P.op("scalar", fcl(lambda e, bb=0, half=0: e.activation(
                        tmpe[:, half:half + 512], ps[bb][:, :], AF.Exp),
                        bb=bb, half=half), waits=ew, inc="s_phE")
                    phEl.append(V("s_phE"))
                    P.op("vector", fcl(lambda e, bb=0, half=0: e.tensor_scalar(
                        tmpt[:, half:half + 512], ps[bb][:, :], 0.0, 1.0,
                        ALU.max, ALU.add), bb=bb, half=half),
                        waits=[("s_kvmm", kvmmv[ti]), ("s_phE", phEl[-1])])
                    pg = [("s_phE", phEl[-1])]
                    if ti == 0:
                        pg += [(f"s_pkst{c % 2}", 16 * (c // 2)),
                               ("s_stA", max(0, c - 1)), ("gones", 2)]
                    P.op("vector", fcl(lambda e, pk=None, half=0: e.tensor_tensor(
                        pk[:, half:half + 512], tmpe[:, half:half + 512],
                        tmpt[:, half:half + 512], ALU.min), pk=pk, half=half),
                        waits=pg, inc="s_kvev")
                for ti in range(2, 4):
                    bb = banks[ti]
                    h0 = (ti - 2) * 8
                    P.op("vector", fcl(lambda e, bb=0, h0=0, pk=None: e.tensor_copy(
                        pk[:, 1024:2064].rearrange("p (h o) -> p h o", o=65)
                        [:, h0:h0 + 8, 0:64],
                        ps[bb][:, :].rearrange("p (h o) -> p h o", o=64)),
                        bb=bb, h0=h0, pk=pk),
                        waits=[("s_kvmm", kvmmv[ti])], inc="s_kvev")
                kvmm_v[c] = kvmmv
                phE_v[c] = phEl[1]
                kvev_v[c] = V("s_kvev")
                bank_free[banks[2]] = [("s_kvev", kvev_v[c])]
                bank_free[banks[3]] = [("s_kvev", kvev_v[c])]
            p1_state(a_chunks - 1)

        if not a_coll:
            P.emit(nc)
            sAC.close()
            return nc
        P.op("vector", lambda e: e.reduce_sum(
            zparts[:, 8:9], zparts[:, 0:8], axis=AX),
            waits=[("s_lgE", V("s_lgE"))], inc="s_zl")
        fin = S_own[(NC - 1) % 2]
        P.op("sync", fcl(lambda e, fin=None: e.dma_start(
            out=ag_in[0:1, 0:66560].rearrange("a (p c) -> (a p) c", p=128),
            in_=fin[:, :]), fin=fin),
            waits=[("s_soadd", V("s_soadd"))], inc="s_agst", inck=16)
        P.op("sync", lambda e: e.dma_start(
            out=ag_in[0:1, 66560:66568].rearrange("a (p c) -> (a p) c", p=8),
            in_=zparts[:, 8:9]),
            waits=[("s_zl", 1)], inc="s_agst", inck=16)
        P.op("gpsimd", lambda e: e.collective_compute(
            "AllGather", ALU.bypass,
            replica_groups=[[0, 1], [2, 3], [4, 5], [6, 7]],
            ins=[ag_in.ap().opt()], outs=[ag_out.ap().opt()]),
            waits=[("s_agst", 32)], inc="s_cc1")
        P.op("gpsimd", lambda e: e.dma_start(
            out=rp[1][:, :],
            in_=ag_out[0:1, 0:66560].rearrange("a (p c) -> (a p) c", p=128)),
            waits=[("s_cc1", 1)], inc="s_agld", inck=16)
        P.op("gpsimd", lambda e: e.dma_start(
            out=zb2[:, :],
            in_=ag_out[0:2, 66560:66568].rearrange("r p -> p r"),
            allow_slow_non_contiguous=True),
            inc="s_agld", inck=16)
        P.op("vector", lambda e: e.tensor_scalar(
            rp[0][:, :], rp[1][:, :], psc128[:, :], None, ALU.mult),
            waits=[("s_agld", 32), ("v_ps", 1)], inc="s_sini")
        P.op("vector", lambda e: e.tensor_copy(S_runb[:, 0:520], rp[0][:, :]),
             inc="s_stcp")
        P.op("vector", lambda e: e.tensor_tensor(
            zb2[:, 0:1], zb2[:, 0:1], zb2[:, 1:2], ALU.add))
        P.op("vector", lambda e: e.reciprocal(rZ[:, :], zb2[:, 0:1]), inc="s_rZ")
        if debug:
            P.op("sync", lambda e: e.dma_start(out=xn_dbg.ap(), in_=xnT[:, :]),
                 waits=[("s_tev0", NC)], inc="s_dbg", inck=16)
        P.emit(nc)

        if stop_phase < 2:
            sAC.close()
            return nc
        P = Prog(book)
        wsrcs = {0: (w1_d, w1b_d), 1: (w2_d, w2b_d), 2: (w3_d, w3b_d)}
        cast_order = [(0, 0), (0, 1), (0, 2), (1, 0)]
        for ei in range(NE):
            if ei + 1 < NE:
                if (ei + 1, 1) not in cast_order:
                    cast_order.append((ei + 1, 1))
                cast_order.append((ei + 1, 2))
            if ei + 2 < NE:
                cast_order.append((ei + 2, 0))
        cast_dmas = []
        for ei, which in cast_order:
            src, dst = wsrcs[which]
            for c8 in range(8):
                cast_dmas.append(fcl(
                    lambda e, ei=0, c8=0, src=None, dst=None: e.dma_start(
                        out=dst[ei, c8 * 128:(c8 + 1) * 128, :],
                        in_=src[ei, c8 * 128:(c8 + 1) * 128, :]),
                    ei=ei, c8=c8, src=src, dst=dst))

        def emit_casts(n):
            while n > 0 and cast_dmas:
                P.op("gpsimd", cast_dmas.pop(0), inc="s_wc", inck=16)
                n -= 1

        wready = {}

        def wslot(ei, which):
            return (3 * ei + which + 2) % 4

        def emit_wload(ei, which, eng, waits):
            slot = wslot(ei, which)
            wsrc = (w1b_d, w2b_d, w3b_d)[which]
            for s in range(8):
                P.op(eng, fcl(lambda e, s=0, slot=0, ei=0, wsrc=None:
                     e.dma_start(out=wpool[slot][:, s * 1024:(s + 1) * 1024],
                                 in_=wsrc[ei, s * 128:(s + 1) * 128, :]),
                     s=s, slot=slot, ei=ei, wsrc=wsrc),
                     waits=(waits if s == 0 else ()),
                     inc=f"s_wl{slot}", inck=16)
            wready[(ei, which)] = (f"s_wl{slot}", V(f"s_wl{slot}"))
        p1f = [("s_phE", V("s_phE")), ("s_kvev", V("s_kvev")),
               ("s_soadd", V("s_soadd"))]
        free_at = {0: list(p1f), 1: list(p1f)}
        free_nm = {2: list(p1f), 3: list(p1f)}
        free_wo = {4: list(p1f), 5: list(p1f)}
        free_st6 = p1f + [("s_lgE", V("s_lgE"))]
        free_psE = [("s_tev0", NC)]
        pkv_free = {0: [("s_pkst0", 16 * (NC // 2)), ("s_stA", NC)],
                    1: [("s_pkst1", 16 * (NC // 2)), ("s_stA", NC)]}
        atm_free = {i: [] for i in range(4)}
        xt_free = {0: [("s_lnf", 31), ("s_sq", 31)],
                   1: [("s_lnf", 32), ("s_sq", 32)]}
        kvld_v = {}
        xld_v = {}

        def load_pkv(c):
            P.op("sync", fcl(lambda e, c=0: e.dma_start(
                out=pkv[c % 2][:, 0:2064], in_=kv_scr[c][:, 0:2064]), c=c),
                waits=[("s_pkst0", 16 * (NC // 2)),
                       ("s_pkst1", 16 * (NC // 2))] + pkv_free[c % 2],
                inc=f"s_kvld{c % 2}", inck=16)
            kvld_v[c] = (f"s_kvld{c % 2}", V(f"s_kvld{c % 2}"))

        def load_xt(c):
            P.op("sync", fcl(lambda e, c=0: e.dma_start(
                out=xt[c % 2][:, :], in_=x_d[c * 128:(c + 1) * 128, :]), c=c),
                waits=xt_free[c % 2], inc=f"s_xt{c % 2}", inck=16)
            xld_v[c] = (f"s_xt{c % 2}", V(f"s_xt{c % 2}"))

        load_pkv(0)
        load_pkv(1)
        load_xt(0)
        load_xt(1)

        prb = p3b[:, 0:32]
        eqb = p3b[:, 32:64]
        c1b = p3b[:, 64:96]
        oh1 = p3b[:, 96:128]
        oh2 = p3b[:, 128:160]
        p2t = p3b[:, 160:192]
        ohcat = p3b[:, 96:160].rearrange("p (b c) -> p b c", c=32)
        ohs = p3b[:, 192:224]
        mx4 = dnm4
        i14 = rec4

        def r3(ap):
            return ap.rearrange("p (k o) -> p k o", o=8)

        def col3(ap):
            return ap.rearrange("p (k o) -> p k o", o=1)

        def bc3(ap):
            return col3(ap).broadcast_to([128, 4, 8])

        P.op("vector", lambda e: e.memset(p3c[:, 0:8], 0.0), inc="s_rsb")
        zfv = V("s_zf")
        Rsb = p3c[:, 0:8]
        possb = p3c[:, 8:16]
        ohb_v = {}

        def emit_p3_a(g):
            P.op("sync", fcl(lambda e, g=0: e.dma_start(
                out=expld[:, :],
                in_=exp_scr[0:8, g * 512:(g + 1) * 512]), g=g),
                waits=[("s_p3a", g), ("s_lgst", 16 * NG)],
                inc="s_p3l", inck=16)
            P.op("vector", lambda e: e.tensor_scalar(
                prT[:, :], expld[:, :], rZ[:, :], None, ALU.mult),
                waits=[("s_p3l", V("s_p3l")), ("s_rZ", 1)], inc="s_p3a")

        def emit_p3_b(g):
            for t in range(4):
                P.op("tensor", fcl(lambda e, t=0: e.transpose(
                    ps[0][:, 128 + t * 8:128 + (t + 1) * 8],
                    prT[:, t * 128:(t + 1) * 128],
                    identf[0:8, 0:8]), t=t),
                    waits=([("s_p3a", V("s_p3a")), ("s_p3b", g), ("vi", 2)]
                           if t == 0 else ()),
                    inc=("s_p3t" if t == 3 else None))
            P.op("vector", lambda e: e.tensor_copy(prb[:, :], ps[0][:, 128:160]),
                 waits=[("s_p3t", V("s_p3t")), ("gi", 4)], inc="s_p3b")
            VV = lambda f, w=(), inc=None: P.op("vector", f, waits=w, inc=inc)
            VV(lambda e: e.tensor_reduce(col3(mx4), r3(prb), AX, ALU.max))
            VV(lambda e: e.tensor_tensor(r3(eqb), r3(prb), bc3(mx4), ALU.is_equal))
            VV(lambda e: e.tensor_tensor(c1b, eqb, iota32[:, :], ALU.mult))
            VV(lambda e: e.tensor_reduce(col3(i14), r3(c1b), AX, ALU.min))
            VV(lambda e: e.tensor_tensor(r3(oh1), r3(iota32[:, :]), bc3(i14),
                                         ALU.is_equal))
            VV(lambda e: e.tensor_tensor(p2t, prb, oh1, ALU.subtract))
            VV(lambda e: e.tensor_reduce(col3(mx4), r3(p2t), AX, ALU.max))
            VV(lambda e: e.tensor_tensor(r3(eqb), r3(p2t), bc3(mx4), ALU.is_equal))
            VV(lambda e: e.tensor_tensor(c1b, eqb, iota32[:, :], ALU.mult))
            VV(lambda e: e.tensor_reduce(col3(i14), r3(c1b), AX, ALU.min))
            VV(lambda e: e.tensor_tensor(r3(oh2), r3(iota32[:, :]), bc3(i14),
                                         ALU.is_equal))
            VV(lambda e: e.tensor_tensor(ohs, oh1, oh2, ALU.add), inc="s_p3w")
            P.op("vector", fcl(lambda e, g=0: e.tensor_copy(
                ohsb[:, :], ohs[:, :]), g=g),
                waits=([("s_pfm", 4 * g), ("s_rmm", 4 * g)]
                       if p3n >= 4 else []), inc="s_ohb")
            ohb_v[g] = V("s_ohb")

        def emit_p3_c(g):
            for t in range(4):
                P.op("tensor", fcl(lambda e, t=0: e.transpose(
                    ps[3][0:8, t * 128:(t + 1) * 128],
                    ohs[:, t * 8:(t + 1) * 8], identf[:, :]), t=t),
                    waits=([("s_p3w", V("s_p3w")), ("s_p3d", 4 * g),
                            ("s_div", V("s_div")), ("s_rq", V("s_rq"))]
                           if t == 0 else ()),
                    inc="s_p3t2")
                P.op("vector", fcl(lambda e, g=0, t=0: e.tensor_copy(
                    indT[:, g * 512 + t * 128:g * 512 + (t + 1) * 128],
                    ps[3][0:8, t * 128:(t + 1) * 128]),
                    g=g, t=t),
                    waits=[("s_p3t2", V("s_p3t2"))], inc="s_p3d")
            free_nm[3] = free_nm[3] + [("s_p3d", V("s_p3d"))]
            for t in range(4):
                P.op("tensor", fcl(lambda e, t=0: e.matmul(
                    ps[0][:, 160 + t * 8:160 + t * 8 + 8],
                    triuSb[:, :], ohsb[:, t * 8:(t + 1) * 8],
                    start=True, stop=True), t=t),
                    waits=[("s_ohb", ohb_v[g]), ("vi", 4),
                           ("s_psb", max(0, 4 * (g - 1) + t + 1))],
                    inc="s_pfm")
                P.op("tensor", fcl(lambda e, t=0: e.matmul(
                    ps[0][:, 192 + t * 8:192 + t * 8 + 8],
                    onesM[:, :], ohsb[:, t * 8:(t + 1) * 8],
                    start=True, stop=True), t=t),
                    waits=[("gi", 7),
                           ("s_rsb", max(0, 1 + 4 * (g - 1) + t + 1)
                            if g > 0 else 0)],
                    inc="s_rmm")

        def emit_p3_tile(g, t):
            T = g * 4 + t
            P.op("vector", fcl(lambda e, t=0: e.tensor_tensor(
                possb, ps[0][:, 160 + t * 8:160 + t * 8 + 8], Rsb, ALU.add),
                t=t),
                waits=[("s_pfm", 4 * g + t + 1), ("s_rsb", V("s_rsb"))],
                inc="s_psb")
            P.op("vector", fcl(lambda e, t=0: e.tensor_tensor(
                Rsb, Rsb, ps[0][:, 192 + t * 8:192 + t * 8 + 8], ALU.add),
                t=t),
                 waits=[("s_rmm", 4 * g + t + 1)], inc="s_rsb")
            tp = T % 2
            P.op("vector", fcl(lambda e, T=0, tp=0: e.tensor_single_scalar(
                tid[:, tp:tp + 1], iotac[:, :], T * 128, ALU.add),
                T=T, tp=tp),
                waits=[("gi", 9),
                       (f"s_sct{tp}", V(f"s_sct{tp}"))],
                inc="s_tid")
            sl = slice(t * 8, (t + 1) * 8)
            tm2 = p3c[:, 16:32].rearrange("p (k o) -> p k o", o=8)
            ohc = ohcat[:, :, sl]

            def bc2(ap):
                return ap.rearrange("p (o c) -> p o c", o=1).broadcast_to(
                    [128, 2, 8])

            def c2(lo):
                return p3c[:, lo:lo + 2].rearrange("p (k o) -> p k o", o=1)

            P.op("vector", fcl(lambda e, ohc=None: e.tensor_tensor(
                tm2, ohc, bc2(possb), ALU.mult), ohc=ohc))
            P.op("vector", lambda e: e.tensor_reduce(c2(32), tm2, AX, ALU.add))
            P.op("vector", fcl(lambda e, ohc=None: e.tensor_tensor(
                tm2, ohc, bc2(iotaEr[:, :]), ALU.mult), ohc=ohc))
            P.op("vector", lambda e: e.tensor_reduce(c2(34), tm2, AX, ALU.add))
            P.op("vector", fcl(lambda e, ohc=None, sl=None: e.tensor_tensor(
                tm2, ohc, bc2(prb[:, sl]), ALU.mult), ohc=ohc, sl=sl))
            P.op("vector", lambda e: e.tensor_reduce(c2(36), tm2, AX, ALU.add))
            P.op("vector", fcl(lambda e, tp=0: e.tensor_copy(
                wt1b[:, tp * 2:tp * 2 + 2], p3c[:, 36:38]), tp=tp))
            P.op("vector", lambda e: e.tensor_single_scalar(
                p3c[:, 38:40], p3c[:, 32:34], float(KCAP), ALU.is_ge))
            P.op("vector", lambda e: e.tensor_copy(
                p3i[:, 0:2], p3c[:, 32:34]))
            P.op("vector", lambda e: e.tensor_single_scalar(
                p3i[:, 2:4], p3i[:, 0:2], 127, ALU.bitwise_and))
            P.op("vector", lambda e: e.tensor_single_scalar(
                p3i[:, 4:6], p3i[:, 0:2], 7, ALU.arith_shift_right))
            P.op("vector", lambda e: e.tensor_copy(
                p3c[:, 40:42], p3i[:, 2:4]))
            P.op("vector", lambda e: e.tensor_copy(
                p3c[:, 42:44], p3i[:, 4:6]))
            P.op("vector", lambda e: e.scalar_tensor_tensor(
                p3c[:, 44:46], p3c[:, 40:42], 12.0, p3c[:, 42:44],
                ALU.mult, ALU.add))
            P.op("vector", lambda e: e.tensor_scalar(
                p3c[:, 46:48], p3c[:, 34:36], float(KCAP), None,
                ALU.mult))
            P.op("vector", lambda e: e.tensor_tensor(
                p3c[:, 46:48], p3c[:, 46:48], p3c[:, 44:46], ALU.add))
            P.op("vector", lambda e: e.scalar_tensor_tensor(
                p3c[:, 46:48], p3c[:, 38:40], BIGOOB, p3c[:, 46:48],
                ALU.mult, ALU.add))
            P.op("vector", fcl(lambda e, tp=0: e.tensor_copy(
                posi[:, tp * 2:tp * 2 + 2], p3c[:, 46:48]),
                tp=tp), inc="s_p3v")
            p3vv = V("s_p3v")
            for k in range(2):
                P.op("gpsimd", fcl(lambda e, k=0, tp=0: e.indirect_dma_start(
                    out=tok_scr.ap().rearrange("e k -> (e k)").unsqueeze(1),
                    out_offset=IndirectOffsetOnAxis(
                        posi[:, tp * 2 + k:tp * 2 + k + 1], 0),
                    in_=tid[:, tp:tp + 1], in_offset=None, oob_is_err=False),
                    k=k, tp=tp),
                    waits=[("s_zf", zfv), ("s_p3v", p3vv),
                           ("s_tid", V("s_tid"))],
                    inc=f"s_sct{tp}", inck=16)
                P.op("gpsimd", fcl(lambda e, k=0, tp=0: e.indirect_dma_start(
                    out=wtc_scr.ap().rearrange("e k -> (e k)").unsqueeze(1),
                    out_offset=IndirectOffsetOnAxis(
                        posi[:, tp * 2 + k:tp * 2 + k + 1], 0),
                    in_=wt1b[:, tp * 2 + k:tp * 2 + k + 1],
                    in_offset=None, oob_is_err=False),
                    k=k, tp=tp),
                    waits=[("s_zf", zfv), ("s_p3v", p3vv)],
                    inc=f"s_sct{tp}", inck=16)
            gd = [("s_psb", V("s_psb")), ("s_rsb", V("s_rsb"))]
            free_wo[4] = free_wo[4] + gd
            free_wo[5] = free_wo[5] + gd

        for g in range(NG):
            qb = [2, 3, 4, 5]
            qev0 = V("s_qev")
            for i in range(16):
                b = qb[i % 4]
                wbuf = wqs if i < 8 else wks
                ft = i % 8
                fw = [("s_qev", qev0 + i - 3)]
                if i < 4:
                    fw += (free_nm[b] if b in (2, 3) else free_wo[b])
                for s in range(8):
                    P.op("tensor", fcl(lambda e, s=0, b=0, wbuf=None, ft=0, g=0:
                         e.matmul(ps[b][:, :],
                                  wbuf[:, s * 1024 + ft * 128: s * 1024 + ft * 128 + 128],
                                  xnT[:, s * NT + g * 512: s * NT + (g + 1) * 512],
                                  start=(s == 0), stop=(s == 7)),
                         s=s, b=b, wbuf=wbuf, ft=ft, g=g),
                        waits=(fw if s == 0 else ()),
                        inc=("s_qmm" if s == 7 else None))
                qmmv = V("s_qmm")
                half = (i % 2) * 512
                dst = (i % 8) * 512 + (0 if i < 8 else 4096)
                ew = [("s_qmm", qmmv), ("s_qev", qev0 + i - 1)]
                if i == 0:
                    ew.append(("s_phf", g) if g > 0 else ("s_kvev", 4 * NC))
                P.op("scalar", fcl(lambda e, b=0, half=0: e.activation(
                    tmpe[:, half:half + 512], ps[b][:, :], AF.Exp),
                    b=b, half=half), waits=ew, inc="s_qphE")
                P.op("vector", fcl(lambda e, b=0, half=0: e.tensor_scalar(
                    tmpt[:, half:half + 512], ps[b][:, :], 0.0, 1.0,
                    ALU.max, ALU.add), b=b, half=half),
                    waits=[("s_qmm", qmmv), ("s_qphE", V("s_qphE"))] +
                          ([("s_phf", g)] if (g > 0 and i == 0) else []))
                P.op("vector", fcl(lambda e, half=0, dst=0: e.tensor_tensor(
                    qkT[:, dst:dst + 512], tmpe[:, half:half + 512],
                    tmpt[:, half:half + 512], ALU.min), half=half, dst=dst),
                    waits=[("s_qphE", V("s_qphE"))], inc="s_qev")
            qphi = V("s_qev")
            free_nm = {2: [("s_qev", qev0 + 13)], 3: [("s_qev", qev0 + 14)]}
            free_wo = {4: [("s_qev", qev0 + 15)], 5: [("s_qev", qev0 + 16)]}

            for cc in range(4):
                c = g * 4 + cc
                tl = cc * 128
                pkvb = pkv[c % 2]
                kvv = kvld_v[c]
                xvv = xld_v[c]
                nmm_prev = V("s_nmm")

                def qo(h, base=0, tl=tl):
                    return base + (h // 2) * 512 + tl

                at_v = {}
                nmm_at_inter = {}

                def emit_AT(p):
                    for h in (2 * p, 2 * p + 1):
                        b = h % 2
                        P.op("tensor", fcl(lambda e, h=0, b=0, q=0, k=0: e.matmul(
                            ps[b][:, 0:128],
                            qkT[(h % 2) * 64:(h % 2) * 64 + 64, k:k + 128],
                            qkT[(h % 2) * 64:(h % 2) * 64 + 64, q:q + 128],
                            start=True, stop=True),
                            h=h, b=b, q=qo(h), k=qo(h, 4096)),
                            waits=[("s_qev", qphi)] + free_at[b],
                            inc="s_at")
                        at_v[h] = V("s_at")

                def emit_mask(p):
                    for h in (2 * p, 2 * p + 1):
                        mw = [("s_at", at_v[h]), ("vi", 3)]
                        if h >= 4:
                            mw += atm_free[h % 4]
                        P.op("vector", fcl(lambda e, h=0: e.tensor_tensor(
                            atm[:, (h % 4) * 128:(h % 4) * 128 + 128],
                            ps[h % 2][:, 0:128], masktri[:, 0:128], ALU.mult), h=h),
                            waits=mw, inc="s_mask")
                        free_at[h % 2] = [("s_mask", V("s_mask"))]

                def emit_numer(p):
                    for h in (2 * p, 2 * p + 1):
                        nb = 2 + ((h // 4) % 2)
                        co = (h % 4) * 65
                        nw = [("s_mask", V("s_mask")), kvv,
                              ("s_stcp", 1 + c)]
                        if h % 4 == 0:
                            nw += free_nm[nb]
                        P.op("tensor", fcl(lambda e, h=0, nb=0, co=0, pkvb=None:
                             e.matmul(ps[nb][:, co:co + 65],
                                      atm[:, (h % 4) * 128:(h % 4) * 128 + 128],
                                      pkvb[:, 1024 + h * 65:1024 + h * 65 + 65],
                                      start=True, stop=False),
                             h=h, nb=nb, co=co, pkvb=pkvb),
                            waits=nw)
                        P.op("tensor", fcl(lambda e, h=0, nb=0, co=0, q=0,
                                           so=0: e.matmul(
                            ps[nb][:, co:co + 65],
                            qkT[(h % 2) * 64:(h % 2) * 64 + 64, q:q + 128],
                            S_runb[(h % 2) * 64:(h % 2) * 64 + 64,
                                   so + (h // 2) * 65:so + (h // 2) * 65 + 65],
                            start=False, stop=True),
                            h=h, nb=nb, co=co, q=qo(h), so=(c % 2) * 520),
                            inc="s_nmm")
                        nmm_at_inter[h] = V("s_nmm")
                        atm_free[h % 4] = [("s_nmm", V("s_nmm"))]

                def emit_state(p):
                    for h in (2 * p, 2 * p + 1):
                        sb = 6 if p < 4 else 5
                        co = (p % 4) * 65
                        sw = [kvv]
                        if h == 0:
                            sw += free_st6
                        if h == 8:
                            sw += free_wo[5]
                        P.op("tensor", fcl(lambda e, h=0, sb=0, co=0, pkvb=None:
                             e.matmul(ps[sb][(h % 2) * 64:(h % 2) * 64 + 64,
                                             co:co + 65],
                                      pkvb[:, h * 64:(h + 1) * 64],
                                      pkvb[:, 1024 + h * 65:1024 + h * 65 + 65],
                                      start=True, stop=True),
                             h=h, sb=sb, co=co, pkvb=pkvb),
                            waits=sw, inc="s_st2")

                def emit_div(grp):
                    nb = 2 + (grp % 2)
                    nv = nmm_at_inter[grp * 4 + 3]
                    P.op("vector", fcl(lambda e, nb=0: e.tensor_single_scalar(
                        dnm4[:, :].rearrange("p (h o) -> p h o", o=1),
                        ps[nb][:, 0:260].rearrange("p (h o) -> p h o", o=65)
                        [:, :, 64:65],
                        EPS_ATTN, ALU.add), nb=nb),
                        waits=[("s_nmm", nv)])
                    P.op("vector", lambda e: e.reciprocal(rec4[:, :], dnm4[:, :]),
                         waits=[("s_div", V("s_div"))], inc="s_rq")
                    for hh in range(4):
                        h = grp * 4 + hh
                        P.op("scalar", fcl(lambda e, h=0, hh=0, nb=0:
                             e.activation(
                                 pre[:, h * 64:(h + 1) * 64],
                                 ps[nb][:, hh * 65:hh * 65 + 64],
                                 AF.Identity, scale=rec4[:, hh:hh + 1]),
                             h=h, hh=hh, nb=nb),
                            waits=[("s_rq", V("s_rq"))], inc="s_div")
                    free_nm[nb] = [("s_div", V("s_div"))]

                emit_AT(0)
                for p in range(8):
                    emit_mask(p)
                    if p + 1 < 8:
                        emit_AT(p + 1)
                    emit_numer(p)
                    emit_state(p)
                    if p % 2 == 1:
                        emit_div(p // 2)
                st2v = V("s_st2")
                cur, nxt = rp[c % 2], rp[(c + 1) % 2]
                P.op("vector", fcl(lambda e, cur=None, nxt=None: e.tensor_tensor(
                    nxt[:, 0:260], cur[:, 0:260], ps[6][:, 0:260], ALU.add),
                    cur=cur, nxt=nxt),
                    waits=[("s_st2", st2v), ("s_sini", 1)])
                P.op("vector", fcl(lambda e, cur=None, nxt=None: e.tensor_tensor(
                    nxt[:, 260:520], cur[:, 260:520], ps[5][:, 0:260], ALU.add),
                    cur=cur, nxt=nxt),
                    inc="s_stadd")
                free_st6 = [("s_stadd", V("s_stadd"))]
                P.op("vector", fcl(lambda e, nxt=None, so=0: e.tensor_copy(
                    S_runb[:, so:so + 520], nxt[:, :]),
                    nxt=nxt, so=((c + 1) % 2) * 520),
                    waits=[("s_nmm", nmm_prev)], inc="s_stcp")
                divv = V("s_div")
                for k in range(8):
                    P.op("tensor", fcl(lambda e, k=0: e.transpose(
                        psE[:, k * 128:(k + 1) * 128], pre[:, k * 128:(k + 1) * 128],
                        identb[:, :]), k=k),
                        waits=(([("s_div", divv)] + free_psE) if k == 0 else ()),
                        inc=("s_tr2" if k == 7 else None))
                P.op("scalar", lambda e: e.activation(
                    preT[:, :], psE[:, :], AF.Identity),
                     waits=[("s_tr2", V("s_tr2"))], inc="s_tev2")
                free_psE = [("s_tev2", V("s_tev2"))]
                tev2 = V("s_tev2")
                for half in range(2):
                    wb = 4 + half
                    ww = [("s_tev2", tev2), ("w_wo", 128)] + free_wo[wb]
                    if half == 1:
                        ww.append(("s_stadd", V("s_stadd")))
                    for s in range(8):
                        P.op("tensor", fcl(lambda e, s=0, wb=0, half=0: e.matmul(
                            ps[wb][:, :],
                            preT[:, s * 128:(s + 1) * 128],
                            wos[:, s * 1024 + half * 512: s * 1024 + half * 512 + 512],
                            start=(s == 0), stop=(s == 7)),
                            s=s, wb=wb, half=half),
                            waits=(ww if s == 0 else ()),
                            inc=("s_wout" if s == 7 else None))
                woutv = V("s_wout")
                P.op("vector", fcl(lambda e, cb=0: e.tensor_tensor(
                    accst[:, 0:512], ps[4][:, :], xt[cb][:, 0:512], ALU.add),
                    cb=c % 2),
                    waits=[("s_wout", woutv - 1), xvv,
                           ("s_acst", 16 * c)])
                P.op("vector", fcl(lambda e, cb=0: e.tensor_tensor(
                    accst[:, 512:1024], ps[5][:, :], xt[cb][:, 512:1024], ALU.add),
                    cb=c % 2),
                    waits=[("s_wout", woutv)], inc="s_acc")
                accv = V("s_acc")
                free_wo = {4: [("s_acc", accv)], 5: [("s_acc", accv)]}
                P.op("gpsimd", fcl(lambda e, c=0: e.dma_start(
                    out=acc_scr[c], in_=accst[:, :]), c=c),
                    waits=[("s_acc", accv)], inc="s_acst", inck=16)
                pkv_free[c % 2] = [("s_nmm", V("s_nmm")), ("s_st2", V("s_st2"))]
                xt_free[c % 2] = [("s_acc", accv)]
                if c + 2 < NC:
                    load_pkv(c + 2)
                    load_xt(c + 2)
                if stop_phase >= 3:
                    if g < 7:
                        if cc == 0 and p3n >= 1:
                            emit_p3_a(g)
                        elif cc == 1 and p3n >= 2:
                            emit_p3_b(g)
                        elif cc == 2 and p3n >= 3:
                            emit_p3_c(g)
                        elif cc == 3 and p3n >= 4:
                            emit_p3_tile(g, 0)
                            emit_p3_tile(g, 1)
                    else:
                        if cc == 0 and p3n >= 2:
                            emit_p3_b(g)
                        elif cc == 1 and p3n >= 3:
                            emit_p3_c(g)
                        elif cc == 2 and p3n >= 4:
                            emit_p3_tile(g, 0)
                            emit_p3_tile(g, 1)
                        elif cc == 3 and p3n >= 4:
                            emit_p3_tile(g, 2)
                            emit_p3_tile(g, 3)
                emit_casts(7)
                if g == 7 and cc == 1:
                    emit_wload(0, 0, "gpsimd",
                               [("s_qmm", V("s_qmm")), ("s_wc", 16 * 8)])
                if g == 7 and cc == 2:
                    emit_wload(1, 0, "gpsimd",
                               [("s_qmm", V("s_qmm")), ("s_wc", 16 * 32)])
            if stop_phase >= 3 and p3n >= 4 and g < 7:
                emit_p3_tile(g, 2)
                emit_p3_tile(g, 3)
                if g == 6 and p3n >= 1:
                    emit_p3_a(7)
            if g == 6:
                emit_wload(0, 1, "gpsimd",
                           [("s_kvmm", V("s_kvmm")), ("s_wc", 16 * 16)])
            P.op("vector", lambda e: e.tensor_copy(lnst[:, 3:4], lnst[:, 2:3]),
                 waits=[("s_tr2", V("s_tr2")), ("s_wout", V("s_wout"))],
                 inc="s_phf")
        P.emit(nc)
        sAC.close()


        if stop_phase < 4:
            return nc
        sDE = ExitStack()
        DD = sDE.enter_context
        xnTc = DD(nc.sbuf_tensor("xnTc", [128, 8192], BF16))
        xng = DD(nc.sbuf_tensor("xng", [128, 4096], BF16))
        idxs = DD(nc.sbuf_tensor("idxs", [128, 24], I32))
        wtcs = DD(nc.sbuf_tensor("wtcs", [128, 24], BF16))
        h1T = DD(nc.sbuf_tensor("h1T", [128, 4096], BF16))
        h2T = DD(nc.sbuf_tensor("h2T", [128, 4096], BF16))
        ybuf = DD(nc.sbuf_tensor("ybuf", [128, 1024], BF16))
        b12s = DD(nc.sbuf_tensor("b12s", [128, 128], F32))
        b3s = DD(nc.sbuf_tensor("b3s", [1, 1024], BF16))
        v32 = DD(nc.sbuf_tensor("v32", [NE, D], F32))
        vstg = DD(nc.sbuf_tensor("vstg", [1, 1024], F32))
        b3bc = DD(nc.sbuf_tensor("b3bc", [128, 1024], F32))
        accld = DD(nc.sbuf_tensor("accld", [128, 4096], F32))
        outst = DD(nc.sbuf_tensor("outst", [128, 4096], F32))
        slots = wpool

        P = Prog(book)
        pcd = [("s_acst", 16 * NC), ("s_wout", V("s_wout")),
               ("s_acc", V("s_acc")), ("s_nmm", V("s_nmm")),
               ("s_qmm", V("s_qmm")), ("s_at", V("s_at")),
               ("s_kvmm", V("s_kvmm")), ("s_lgmm", V("s_lgmm")),
               ("s_st2", V("s_st2")), ("s_tr2", V("s_tr2")),
               ("s_tr0", V("s_tr0")), ("s_sq", V("s_sq")),
               ("s_p3d", V("s_p3d")), ("s_p3t", V("s_p3t")), ("s_p3t2", V("s_p3t2")),
               ("s_p3a", V("s_p3a")), ("s_agst", 32), ("s_sini", 1),
               ("s_lgst", 16 * NG), ("s_p3l", V("s_p3l")),
               ("s_xns0", 16 * (NC // 2)), ("s_xns1", 16 * (NC // 2)),
               ("s_kvld0", V("s_kvld0")),
               ("s_kvld1", V("s_kvld1")), ("s_xt0", V("s_xt0")),
               ("s_xt1", V("s_xt1")), ("s_pkst0", 16 * (NC // 2)),
               ("s_pkst1", 16 * (NC // 2))]
        P.op("sync", lambda e: e.dma_start(out=b12s[:, :], in_=b12_d[:, :]),
             waits=pcd, inc="s_bld", inck=16)
        def load_b3(ei, waits=()):
            P.op("gpsimd", fcl(lambda e, ei=0: e.dma_start(
                out=b3s[:, :], in_=b3_d[0:1, ei * 1024:(ei + 1) * 1024]), ei=ei),
                waits=waits, inc="s_bl3", inck=16)
        load_b3(0, waits=pcd)

        slot_free = {
            0: [("s_nmm", V("s_nmm")), ("s_at", V("s_at"))],
            1: [("s_qmm", V("s_qmm"))],
            2: [("s_qmm", V("s_qmm"))],
            3: [("s_kvmm", V("s_kvmm"))],
        }

        def load_w(ei, which, wsrc):
            slot = wslot(ei, which)
            for s in range(8):
                P.op("sync", fcl(lambda e, s=0, slot=0, ei=0, wsrc=None:
                     e.dma_start(out=slots[slot][:, s * 1024:(s + 1) * 1024],
                                 in_=wsrc[ei, s * 128:(s + 1) * 128, :]),
                     s=s, slot=slot, ei=ei, wsrc=wsrc),
                     waits=((slot_free[slot] +
                             [("s_wc", 16 * 8 *
                               (cast_order.index((ei, which)) + 1))])
                            if s == 0 else ()),
                     inc=f"s_wl{slot}", inck=16)
            wready[(ei, which)] = (f"s_wl{slot}", V(f"s_wl{slot}"))

        load_w(0, 2, w3b_d)

        idx_v = {}
        wtc_v = {}

        def load_idx(e_):
            eb = e_ % 2
            gw_ = [("s_sct0", 16 * 64), ("s_sct1", 16 * 64)] + \
                (pcd if e_ < 2 else []) + \
                (gth_hist[e_ - 2] if e_ >= 2 else [])
            P.op("sync", fcl(lambda e, e_=0, eb=0: e.dma_start(
                out=idxs[:, eb * 12:(eb + 1) * 12],
                in_=tok_scr[e_:e_ + 1, :].rearrange(
                    "a (t k) -> (a t) k", k=12)), e_=e_, eb=eb),
                waits=gw_, inc=f"s_idx{eb}", inck=16)
            idx_v[e_] = (f"s_idx{eb}", V(f"s_idx{eb}"))
            vw_ = [("s_sct0", 16 * 64), ("s_sct1", 16 * 64)] + \
                (pcd if e_ < 2 else []) + \
                ([("s_vmm", vmm_hist[e_ - 2])] if e_ >= 2 else [])
            P.op("sync", fcl(lambda e, e_=0, eb=0: e.dma_start(
                out=wtcs[:, eb * 12:(eb + 1) * 12],
                in_=wtc_scr[e_:e_ + 1, :].rearrange(
                    "a (t k) -> (a t) k", k=12)), e_=e_, eb=eb),
                waits=vw_, inc=f"s_wtc{eb}", inck=16)
            wtc_v[e_] = (f"s_wtc{eb}", V(f"s_wtc{eb}"))

        gth_hist = {}
        vmm_hist = {}
        l1_hist = {}
        tec_v = {}

        def emit_gt_dma(bi):
            e_, tb_ = divmod(bi, NTB)
            for kk in range(4):
                ki = tb_ * 4 + kk
                gb = kk
                gw_ = [idx_v[e_],
                       ("s_xns0", 16 * (NC // 2)), ("s_xns1", 16 * (NC // 2)),
                       ("s_trc", max(0, 4 * bi + kk - 3))]
                P.op("gpsimd", fcl(lambda e, e_=0, ki=0, gb=0, eb=0:
                     e.indirect_dma_start(
                         out=xng[:, gb * 1024:(gb + 1) * 1024],
                         out_offset=None,
                         in_=xn_tm.ap(),
                         in_offset=IndirectOffsetOnAxis(
                             idxs[:, eb * 12 + ki:eb * 12 + ki + 1], 0),
                         oob_is_err=False),
                     e_=e_, ki=ki, gb=gb, eb=e_ % 2),
                    waits=gw_, inc=f"s_gth{kk}", inck=16)
            gth_hist[e_] = [(f"s_gth{kk}", V(f"s_gth{kk}"))
                            for kk in range(4)]

        def emit_gt_pe(bi):
            buf = bi % 2
            for kk in range(4):
                gb = kk
                gthv = (f"s_gth{kk}", 16 * (bi + 1))
                for k in range(8):
                    P.op("tensor", fcl(lambda e, k=0, gb=0: e.transpose(
                        psE[:, k * 128:(k + 1) * 128],
                        xng[:, gb * 1024 + k * 128: gb * 1024 + (k + 1) * 128],
                        identb[:, :]), k=k, gb=gb),
                        waits=([gthv, ("s_tec", V("s_tec")),
                                ("s_tev2", V("s_tev2")), ("s_tev0", NC)]
                               if k == 0 else ()),
                        inc=("s_trc" if k == 7 else None))
                P.op("vector", fcl(lambda e, buf=0, kk=0: e.tensor_copy(
                    xnTc[:, buf * 4096:(buf + 1) * 4096].rearrange(
                        "p (s t) -> p s t", s=8)[:, :, kk * 128:(kk + 1) * 128],
                    psE[:, :].rearrange("p (s t) -> p s t", t=128)),
                    buf=buf, kk=kk),
                    waits=[("s_trc", V("s_trc"))] +
                          ([("s_l1", l1_hist.get(bi - 2, 0))] if kk == 0 else []),
                    inc="s_tec")
            tec_v[bi] = V("s_tec")

        load_idx(0)
        load_idx(1)

        rot = [0, 1, 2, 3, 4]
        bank_ev = {0: [("s_mask", V("s_mask"))],
                   1: [("s_mask", V("s_mask"))],
                   2: [("s_p3d", V("s_p3d")), ("s_p3t", V("s_p3t")),
                       ("s_p3t2", V("s_p3t2"))],
                   3: [("s_div", V("s_div")), ("s_p3d", V("s_p3d"))],
                   4: [("s_acc", V("s_acc"))]}
        vfree = [[("s_acc", V("s_acc")), ("s_stadd", V("s_stadd"))],
                 [("s_stadd", V("s_stadd"))]]
        ri = 0
        vmm_pending = []
        h1_free = list(pcd)
        h2_free = list(pcd)
        ybuf_free = {0: list(pcd), 1: list(pcd)}

        emit_gt_dma(0)
        emit_gt_dma(1)
        emit_gt_pe(0)
        for ei in range(NE):
            s1 = slots[wslot(ei, 0)]
            s2 = slots[wslot(ei, 1)]
            s3 = slots[wslot(ei, 2)]
            for half in range(2):
                bb = rot[ri % 5]
                ri += 1
                P.op("tensor", fcl(lambda e, bb=0, half=0: e.matmul(
                    ps[bb][:, :], onesrow[:, :],
                    b3s[:, half * 512:half * 512 + 512],
                    start=True, stop=True), bb=bb, half=half),
                    waits=[("s_bl3", 16 * (ei + 1)), ("gi", 5)] + bank_ev[bb],
                    inc="s_b3m")
                P.op("vector", fcl(lambda e, bb=0, half=0: e.tensor_copy(
                    b3bc[:, half * 512:half * 512 + 512], ps[bb][:, :]),
                    bb=bb, half=half),
                    waits=[("s_b3m", V("s_b3m"))] +
                          ([("s_e3", V("s_e3"))] if ei > 0 else []),
                    inc="s_b3c")
                bank_ev[bb] = [("s_b3c", V("s_b3c"))]
            b3v = V("s_b3c")
            for tb in range(NTB):
                bi = ei * NTB + tb
                buf = bi % 2
                xoff = buf * 4096
                if bi + 2 < NE * NTB:
                    emit_gt_dma(bi + 2)
                for ot in range(8):
                    b = rot[ri % 5]
                    ri += 1
                    for s in range(8):
                        P.op("tensor", fcl(lambda e, s=0, b=0, ot=0, s1=None,
                                           xoff=0: e.matmul(
                             ps[b][:, :],
                             s1[:, s * 1024 + ot * 128: s * 1024 + ot * 128 + 128],
                             xnTc[:, xoff + s * 512: xoff + (s + 1) * 512],
                             start=(s == 0), stop=(s == 7)),
                             s=s, b=b, ot=ot, s1=s1, xoff=xoff),
                            waits=(([wready[(ei, 0)], ("s_tec", tec_v[bi])]
                                    + bank_ev[b])
                                   if s == 0 else ()),
                            inc=("s_l1" if s == 7 else None))
                    P.op("scalar", fcl(lambda e, b=0, ot=0, ei=0: e.activation(
                        h1T[:, ot * 512:(ot + 1) * 512], ps[b][:, :], AF.Relu,
                        bias=b12s[:, ei * 8 + ot:ei * 8 + ot + 1]),
                        b=b, ot=ot, ei=ei),
                        waits=[("s_l1", V("s_l1")), ("s_bld", 16)] + h1_free,
                        inc="s_e1")
                    bank_ev[b] = [("s_e1", V("s_e1"))]
                h1v = V("s_e1")
                l1_hist[bi] = V("s_l1")
                if bi + 1 < NE * NTB:
                    emit_gt_pe(bi + 1)
                for ot in range(8):
                    b = rot[ri % 5]
                    ri += 1
                    for s in range(8):
                        P.op("tensor", fcl(lambda e, s=0, b=0, ot=0, s2=None:
                             e.matmul(ps[b][:, :],
                                      s2[:, s * 1024 + ot * 128: s * 1024 + ot * 128 + 128],
                                      h1T[:, s * 512:(s + 1) * 512],
                                      start=(s == 0), stop=(s == 7)),
                             s=s, b=b, ot=ot, s2=s2),
                            waits=(([("s_e1", h1v), wready[(ei, 1)]]
                                    + bank_ev[b]) if s == 0 else ()),
                            inc=("s_l2" if s == 7 else None))
                    P.op("scalar", fcl(lambda e, b=0, ot=0, ei=0: e.activation(
                        h2T[:, ot * 512:(ot + 1) * 512], ps[b][:, :], AF.Relu,
                        bias=b12s[:, 64 + ei * 8 + ot:64 + ei * 8 + ot + 1]),
                        b=b, ot=ot, ei=ei),
                        waits=[("s_l2", V("s_l2"))] + h2_free, inc="s_e2")
                    bank_ev[b] = [("s_e2", V("s_e2"))]
                h2v = V("s_e2")
                h1_free = [("s_l2", V("s_l2"))]
                for t4 in range(4):
                    for half in range(2):
                        b = rot[ri % 5]
                        ri += 1
                        for s in range(8):
                            P.op("tensor", fcl(lambda e, s=0, b=0, t4=0, half=0,
                                               s3=None: e.matmul(
                                 ps[b][:, :],
                                 h2T[:, s * 512 + t4 * 128: s * 512 + t4 * 128 + 128],
                                 s3[:, s * 1024 + half * 512: s * 1024 + half * 512 + 512],
                                 start=(s == 0), stop=(s == 7)),
                                 s=s, b=b, t4=t4, half=half, s3=s3),
                                waits=(([("s_e2", h2v),
                                         wready[(ei, 2)]] + bank_ev[b])
                                       if s == 0 else ()),
                                inc=("s_l3" if s == 7 else None))
                        P.op("vector", fcl(lambda e, b=0, half=0:
                             e.tensor_tensor(
                                 ybuf[:, half * 512:half * 512 + 512],
                                 ps[b][:, :],
                                 b3bc[:, half * 512:half * 512 + 512], ALU.add),
                             b=b, half=half),
                            waits=[("s_l3", V("s_l3")), ("s_b3c", b3v)]
                                  + ybuf_free[half],
                            inc="s_ya")
                        P.op("scalar", fcl(lambda e, half=0: e.activation(
                            ybuf[:, half * 512:half * 512 + 512],
                            ybuf[:, half * 512:half * 512 + 512],
                            AF.Relu), half=half),
                            waits=[("s_ya", V("s_ya"))],
                            inc="s_e3")
                        bank_ev[b] = [("s_e3", V("s_e3"))]
                        first = (tb == 0 and t4 == 0)
                        last = (tb == NTB - 1 and t4 == 3)
                        vw = [("s_e3", V("s_e3")), wtc_v[ei]]
                        if first:
                            vw += [("s_vev", V("s_vev"))] + vfree[half]
                        vmm_pending.append(
                            (fcl(lambda e, vb=0, half=0, ki=0, eb=0,
                                 first=False, last=False: e.matmul(
                                     ps[vb][0:1, 0:512],
                                     wtcs[:, eb * 12 + ki:eb * 12 + ki + 1],
                                     ybuf[:, half * 512:half * 512 + 512],
                                     start=first, stop=last),
                                 vb=5 + half, half=half, ki=tb * 4 + t4,
                                 eb=ei % 2, first=first, last=last),
                             vw, half))
                        if len(vmm_pending) > 1:
                            fn0, vw0, half0 = vmm_pending.pop(0)
                            P.op("tensor", fn0, waits=vw0, inc="s_vmm")
                            ybuf_free[half0] = [("s_vmm", V("s_vmm"))]
                h2_free = [("s_l3", V("s_l3"))]
            while vmm_pending:
                fn0, vw0, half0 = vmm_pending.pop(0)
                P.op("tensor", fn0, waits=vw0, inc="s_vmm")
                ybuf_free[half0] = [("s_vmm", V("s_vmm"))]
            vmm_hist[ei] = V("s_vmm")
            vmmv = V("s_vmm")
            P.op("vector", lambda e: e.tensor_copy(vstg[:, 0:512], ps[5][0:1, 0:512]),
                 waits=[("s_vmm", vmmv), ("s_vst", 16 * ei)])
            P.op("vector", lambda e: e.tensor_copy(vstg[:, 512:1024],
                                                   ps[6][0:1, 0:512]),
                 inc="s_vev")
            P.op("sync", fcl(lambda e, ei=0: e.dma_start(
                out=v_in[ei:ei + 1, :], in_=vstg[:, :]), ei=ei),
                waits=[("s_vev", V("s_vev"))], inc="s_vst", inck=16)
            if ei == 5:
                P.op("gpsimd", lambda e: e.collective_compute(
                    "AllReduce", ALU.add,
                    replica_groups=[[0, 1, 2, 3, 4, 5, 6, 7]],
                    ins=[v_in[0:6, :].opt()], outs=[v_out[0:6, :].opt()]),
                    waits=[("s_vst", 16 * 6)], inc="s_cc2")
                P.op("sync", lambda e: e.dma_start(
                    out=v32[0:6, :], in_=v_out[0:6, :]),
                    waits=[("s_cc2", 1)], inc="s_vld", inck=16)
            if ei == 6:
                P.op("gpsimd", lambda e: e.collective_compute(
                    "AllReduce", ALU.add,
                    replica_groups=[[0, 1, 2, 3, 4, 5, 6, 7]],
                    ins=[v_in[6:7, :].opt()], outs=[v_out[6:7, :].opt()]),
                    waits=[("s_vst", 16 * 7)], inc="s_cc2")
                P.op("sync", lambda e: e.dma_start(
                    out=v32[6:7, :], in_=v_out[6:7, :]),
                    waits=[("s_cc2", 2)], inc="s_vld", inck=16)
            vfree = [[], []]
            slot_free[wslot(ei, 0)] = [("s_l1", V("s_l1"))]
            slot_free[wslot(ei, 1)] = [("s_l2", V("s_l2"))]
            slot_free[wslot(ei, 2)] = [("s_l3", V("s_l3"))]
            if ei + 1 < NE:
                load_b3(ei + 1, waits=[("s_b3m", V("s_b3m"))])
                if (ei + 1, 1) not in wready:
                    load_w(ei + 1, 1, w2b_d)
                load_w(ei + 1, 2, w3b_d)
            if ei + 2 < NE:
                load_w(ei + 2, 0, w1b_d)
                load_idx(ei + 2)

        RQ = ("gpsimd", "sync", "scalar")
        ald_v = {}

        def emit_accld(t):
            eb = t % 4
            rr = t % 3
            P.op(RQ[rr], fcl(lambda e, t=0, eb=0: e.dma_start(
                out=accld[:, eb * 1024:(eb + 1) * 1024], in_=acc_scr[t]),
                t=t, eb=eb),
                waits=[("s_o5", 2 * t - 6), ("s_acst", 16 * NC)] +
                      (pcd if t < 4 else []),
                inc=f"s_aldr{rr}", inck=16)
            ald_v[t] = (f"s_aldr{rr}", V(f"s_aldr{rr}"))

        for t in range(4):
            emit_accld(t)
        P.op("gpsimd", lambda e: e.collective_compute(
            "AllReduce", ALU.add,
            replica_groups=[[0, 1, 2, 3, 4, 5, 6, 7]],
            ins=[v_in[7:8, :].opt()], outs=[v_out[7:8, :].opt()]),
            waits=[("s_vst", 16 * NE)], inc="s_cc2")
        P.op("sync", lambda e: e.dma_start(out=v32[7:8, :], in_=v_out[7:8, :]),
             waits=[("s_cc2", 3)], inc="s_vld", inck=16)
        P.op("vector", lambda e: e.tensor_copy(vsb[:, :], v32[:, :]),
            waits=[("s_vld", 48)], inc="s_vsb")
        P.emit(nc)

        if stop_phase < 5:
            sDE.close()
            return nc
        P.emit_marker = None
        p4d = [("s_l1", V("s_l1")), ("s_l2", V("s_l2")), ("s_l3", V("s_l3")),
               ("s_e1", V("s_e1")), ("s_e2", V("s_e2")), ("s_e3", V("s_e3")),
               ("s_vmm", V("s_vmm")), ("s_vev", V("s_vev")), ("s_vsb", 1),
               ("s_vld", 48), ("s_bld", 16), ("s_bl3", V("s_bl3"))] +               [(f"s_wl{i}", V(f"s_wl{i}")) for i in range(5)]
        P = Prog(book)
        ost_v = {}
        for t in range(NC):
            eb = t % 4
            rr = t % 3
            for half in range(2):
                b = (t % 2) * 2 + half
                mw = [("s_vsb", 1), ("s_p3d", V("s_p3d")), ("s_o5", 2 * t - 2)]
                if t < 2:
                    mw += bank_ev[b]
                P.op("tensor", fcl(lambda e, t=0, half=0, b=0: e.matmul(
                    ps[b][:, :], indT[:, t * 128:(t + 1) * 128],
                    vsb[:, half * 512:(half + 1) * 512],
                    start=True, stop=True), t=t, half=half, b=b),
                    waits=mw, inc="s_m5")
                ow = [("s_m5", V("s_m5")), ald_v[t]]
                if t >= 4:
                    ow.append(ost_v[t - 4])
                if t == 0:
                    ow += p4d
                P.op("vector", fcl(lambda e, eb=0, half=0, b=0: e.tensor_tensor(
                    outst[:, eb * 1024 + half * 512: eb * 1024 + half * 512 + 512],
                    ps[b][:, :],
                    accld[:, eb * 1024 + half * 512: eb * 1024 + half * 512 + 512],
                    ALU.add), eb=eb, half=half, b=b),
                    waits=ow, inc="s_o5")
            P.op(RQ[rr], fcl(lambda e, t=0, eb=0: e.dma_start(
                out=out_d[t * 128:(t + 1) * 128, :],
                in_=outst[:, eb * 1024:(eb + 1) * 1024]), t=t, eb=eb),
                waits=[("s_o5", V("s_o5"))], inc=f"s_ostr{rr}", inck=16)
            ost_v[t] = (f"s_ostr{rr}", V(f"s_ostr{rr}"))
            if t + 4 < NC:
                emit_accld(t + 4)
        for rr in range(3):
            P.op("gpsimd", fcl(lambda e, rr=0: e.wait_ge(
                book.sem(f"s_ostr{rr}"), book.v(f"s_ostr{rr}")), rr=rr),
                bare=True)
        P.emit(nc)
        sDE.close()

    return nc


_NC_CACHE = {}


def kernel(**inputs):
    x = np.asarray(inputs["L0_attn_x"], dtype=np.float32)
    wqkv = np.asarray(inputs["L0_attn_W_qkv"], dtype=np.float32)
    wo = np.ascontiguousarray(np.asarray(inputs["L0_attn_W_out"], dtype=np.float32))
    gw = np.ascontiguousarray(np.asarray(inputs["L0_ffn_gate_w"], dtype=np.float32))
    gb = np.ascontiguousarray(
        np.asarray(inputs["L0_ffn_gate_b"], dtype=np.float32).reshape(8, 1))
    w1 = np.ascontiguousarray(np.asarray(inputs["L0_ffn_w1"], dtype=np.float32))
    w2 = np.ascontiguousarray(np.asarray(inputs["L0_ffn_w2"], dtype=np.float32))
    w3 = np.ascontiguousarray(np.asarray(inputs["L0_ffn_w3"], dtype=np.float32))
    b1 = np.asarray(inputs["L0_ffn_b1"], dtype=np.float32)
    b2 = np.asarray(inputs["L0_ffn_b2"], dtype=np.float32)
    b3 = np.ascontiguousarray(
        np.asarray(inputs["L0_ffn_b3"], dtype=np.float32).reshape(1, 8192))

    wq = np.ascontiguousarray(wqkv[:, 0:1024])
    wk = np.ascontiguousarray(wqkv[:, 1024:2048])
    wv = np.ascontiguousarray(wqkv[:, 2048:3072])
    b12 = np.ascontiguousarray(
        np.stack([b1, b2]).reshape(2, 8, 8, 128).transpose(3, 0, 1, 2).reshape(128, 128))

    if "nc" not in _NC_CACHE:
        _NC_CACHE["nc"] = build_nc()
    nc = _NC_CACHE["nc"]

    in_maps = []
    for c in range(8):
        beta, half = c >> 1, c & 1
        xs = np.ascontiguousarray(x[beta, half * NT:(half + 1) * NT, :])
        in_maps.append({
            "x": xs, "wq": wq, "wk": wk, "wv": wv, "wo": wo,
            "gw": gw, "gb": gb, "w1": w1, "w2": w2, "w3": w3,
            "b12": b12, "b3": b3,
            "psc": np.array([[float(half)]], dtype=np.float32),
        })

    res = run_bass_kernel_spmd(nc, in_maps, core_ids=list(range(8)))
    out = np.empty((4, 8192, 1024), dtype=np.float32)
    for c in range(8):
        beta, half = c >> 1, c & 1
        out[beta, half * NT:(half + 1) * NT, :] = res.results[c]["out"]
    return out
